# version 25
# baseline (speedup 1.0000x reference)
import numpy as np
import ml_dtypes
from contextlib import ExitStack

import concourse.bass as bass
import concourse.mybir as mybir
from concourse.bass_utils import run_bass_kernel_spmd

F32 = mybir.dt.float32
BF16 = mybir.dt.bfloat16
ALU = mybir.AluOpType
AF = mybir.ActivationFunctionType
AX = mybir.AxisListType.X

NCORES = 8
N = 2048
D = 1024
DC = 8
QC = 24
H = 2048
HC = 16
E = 8
TA = 512
ATILES = N // TA
TOKC = N // 128
HALF = N // 2
EPS = 1e-5
NEG = -1.0e30

SW0, SB0, IW0, IB0, BIN0, B10, GW0, GB0 = 0, 72, 96, 128, 136, 160, 288, 352
NCONST = 360

ARENA = 28672
A_HB = 0
A_W1 = 8192
A_W2 = 16384
A_WBC = 24576
A_LN2F = 24576
A_LN2T = 25600
A_WIN = 0
A_WOUT = 12288
A_QKV = 16384
A_V2 = 22624
A_TMPA = 24704
A_TMPB = 24964
A_Y = 25224
A_LN1T = 26248
A_LN1F = 27272


class Sem:
    def __init__(self, handle):
        self.h = handle
        self.n = 0

    def inc(self, ins, k=1):
        ins.then_inc(self.h, k)
        self.n += k
        return self.n


class Waiter:
    def __init__(self, eng):
        self.eng = eng
        self.seen = {}

    def wait(self, sem, val):
        if not val or val <= 0:
            return
        assert val <= sem.n, f"wait ahead of emission: {val} > {sem.n}"
        if self.seen.get(id(sem), -1) >= val:
            return
        self.eng.wait_ge(sem.h, val)
        self.seen[id(sem)] = val


def build_program():
    nc = bass.Bass("TRN2", target_bir_lowering=False,
                   dynamic_dma_scratch_size=8192)

    d_u = nc.dram_tensor("u", [N, D], F32, kind="ExternalInput").ap()
    d_out = nc.dram_tensor("out", [N, D], F32, kind="ExternalOutput").ap()
    d_win = nc.dram_tensor("winl", [128, DC * 3072], F32, kind="ExternalInput").ap()
    d_wout = nc.dram_tensor("woutl", [128, DC * 1024], F32, kind="ExternalInput").ap()
    d_w1 = nc.dram_tensor("w1l", [E, 128, DC * H], F32, kind="ExternalInput").ap()
    d_w2 = nc.dram_tensor("w2l", [E, 128, HC * 1024], F32, kind="ExternalInput").ap()
    d_cf = nc.dram_tensor("constsf", [128, NCONST + 128], F32, kind="ExternalInput").ap()
    d_b2 = nc.dram_tensor("b2r", [8, 1024], F32, kind="ExternalInput").ap()
    d_bout = nc.dram_tensor("boutr", [1, 1024], F32, kind="ExternalInput").ap()
    d_sel = nc.dram_tensor("selb", [8, 1152], BF16, kind="ExternalInput").ap()

    ctx = ExitStack()
    sb = lambda name, shape, dt: ctx.enter_context(nc.sbuf_tensor(name, shape, dt))

    resid = sb("resid", [128, TOKC * 1024], F32)
    ln2T_t = sb("ln2T", [128, DC * 1024], BF16)
    wgtT = sb("wgtT", [8, HALF], BF16)
    constsf_t = sb("constsf_s", [128, NCONST + 128], F32)
    constsf = constsf_t[:, 0:NCONST]
    identf = constsf_t[:, NCONST:NCONST + 128]
    b2B = sb("b2B", [8, 1024], BF16)
    boutB = sb("boutB", [1, 1024], BF16)
    selB_t = sb("selB", [8, 1152], BF16)
    selB = selB_t[:, 0:1024]
    onesB = selB_t[0:1, 1024:1152]
    g_l8 = sb("g_l8", [128, 64], F32)
    g_e = sb("g_e", [128, 64], F32)
    g_e2 = sb("g_e2", [128, 64], F32)
    g_pr = sb("g_pr", [128, 64], F32)
    g_l2 = sb("g_l2", [128, 64], F32)
    g_l2b = sb("g_l2b", [128, 64], F32)
    g_nb = sb("g_nb", [128, 8], F32)
    g_wgt = sb("g_wgt", [128, 64], F32)
    g_m1 = sb("g_m1", [128, 8], F32)
    g_m2 = sb("g_m2", [128, 8], F32)
    g_sum = sb("g_sum", [128, 8], F32)
    tmpC = sb("tmpC", [128, 512], F32)
    s_bn6 = sb("s_bn6", [128, 96], F32)
    s_bn = sb("s_bn", [128, 16], F32)
    s_var = sb("s_var", [128, 8], F32)
    s_lnv = sb("s_lnv", [128, 8], F32)
    s_inv = sb("s_inv", [128, 8], F32)
    s_nmi = sb("s_nmi", [128, 8], F32)

    psum = ctx.enter_context(nc.psum_tensor("psum", [128, 4096], F32))

    ARENA_BYTES = 123392
    ab = (nc.sbuf_top - ARENA_BYTES) // 32 * 32
    assert ab >= nc.sbuf_base, (ab, nc.sbuf_base, nc.sbuf_top)
    mt = lambda name, shape, dt, off: nc.alloc_sbuf_tensor_at(
        name, shape, dt, offset=ab + off).ap()
    w1bt = mt("w1bt", [128, 16384], BF16, 0)
    hbt = mt("hbt", [128, 16384], BF16, 32768)
    w2bt = mt("w2bt", [128, 16384], BF16, 65536)
    wbct = mt("wbct", [128, 8192], BF16, 98304)
    ln2ft = mt("ln2ft", [128, 1024], F32, 107008)
    ln2tft = mt("ln2tft", [128, 1024], F32, 111104)
    ln2ft2 = mt("ln2ft2", [128, 1024], F32, 115200)
    ln2tft2 = mt("ln2tft2", [128, 1024], F32, 119296)
    winbt = mt("winbt", [128, 24576], BF16, 0)
    woutbt = mt("woutbt", [128, 8192], BF16, 49152)
    qkvt = mt("qkvt", [128, 12480], BF16, 65536)
    v2t = mt("v2t", [128, 4160], BF16, 90496)
    tmpA = mt("tmpAt", [128, 512], F32, 98816)
    tmpB = mt("tmpBt", [128, 512], F32, 100864)
    ln1f = mt("ln1ft", [128, 1024], F32, 102912)
    ln1tt = mt("ln1tt", [128, 4096], BF16, 107008)
    ybt = mt("ybt", [128, 4096], BF16, 115200)

    def bank(i, w=512, off=0):
        return psum[:, 512 * i + off:512 * i + off + w]

    hbuf = [hbt[:, 1024 * i:1024 * (i + 1)] for i in range(HC)]
    w1B = [w1bt[:, 2048 * i:2048 * (i + 1)] for i in range(DC)]
    w2B = [w2bt[:, 1024 * i:1024 * (i + 1)] for i in range(HC)]
    wbc = [wbct[:, 1024 * i:1024 * (i + 1)] for i in range(E)]
    w1_region = w1bt
    w2_region = w2bt
    win_region = winbt
    wout_region = woutbt

    ln2tfA = [ln2tft[:, 128 * i:128 * (i + 1)] for i in range(DC)]
    ln2tfB = [ln2tft2[:, 128 * i:128 * (i + 1)] for i in range(DC)]
    ln2T = [ln2T_t[:, 1024 * i:1024 * (i + 1)] for i in range(DC)]
    winB = [winbt[:, 3072 * i:3072 * (i + 1)] for i in range(DC)]
    woutB = [woutbt[:, 1024 * i:1024 * (i + 1)] for i in range(DC)]
    qkvraw = [qkvt[:, 520 * i:520 * (i + 1)] for i in range(QC)]
    v2 = [v2t[:, 520 * i:520 * (i + 1)] for i in range(DC)]
    ybuf = [ybt[:, 512 * i:512 * (i + 1)] for i in range(DC)]
    ln1T = [ln1tt[:, 512 * i:512 * (i + 1)] for i in range(DC)]

    s_t = Sem(ctx.enter_context(nc.semaphore("s_t")))
    s_v = Sem(ctx.enter_context(nc.semaphore("s_v")))
    s_s = Sem(ctx.enter_context(nc.semaphore("s_s")))
    s_gw = Sem(ctx.enter_context(nc.semaphore("s_gw")))
    s_su = Sem(ctx.enter_context(nc.semaphore("s_su")))
    s_o = Sem(ctx.enter_context(nc.semaphore("s_o")))
    s_gm = Sem(ctx.enter_context(nc.semaphore("s_gm")))
    s_k = Sem(ctx.enter_context(nc.semaphore("s_k")))
    s_wd = [Sem(ctx.enter_context(nc.semaphore(f"s_wd{i}"))) for i in range(DC)]
    s_u0 = Sem(ctx.enter_context(nc.semaphore("s_u0")))
    s_u1 = Sem(ctx.enter_context(nc.semaphore("s_u1")))

    block = ctx.enter_context(nc.Block())

    @block.sync
    def _(sy):
        V, S, T, G = nc.vector, nc.scalar, nc.tensor, nc.gpsimd
        wV, wS, wT, wG, wSy = Waiter(V), Waiter(S), Waiter(T), Waiter(G), Waiter(sy)

        pre0 = s_gm.inc(G.memset(s_var[:, 0:1], 0.0), 1)
        ua = s_u0.inc(sy.dma_start(
            resid[:, 0:4096].rearrange("p (i d) -> p i d", d=1024),
            d_u.rearrange("(i p) d -> p i d", p=128)[:, 0:4]), 16)
        s_k.inc(sy.dma_start(constsf_t[:, :], d_cf), 16)
        s_k.inc(sy.dma_start(selB_t[:, :], d_sel), 16)
        k_done = s_k.n
        ub = s_u1.inc(sy.dma_start(
            resid[:, 4096:16384].rearrange("p (i d) -> p i d", d=1024),
            d_u.rearrange("(i p) d -> p i d", p=128)[:, 4:16]), 16)
        wS.wait(s_gm, pre0)
        preload_mark = s_s.inc(S.activation(s_lnv[:, 0:1], s_var[:, 0:1],
                                            AF.Exp))

        for qc in range(QC):
            s_gm.inc(G.memset(qkvraw[qc][:, 0:8], 0.0), 1)
        for c in range(DC):
            s_gm.inc(G.memset(v2[c][:, 0:8], 0.0), 1)
        halo0_done = s_gm.inc(G.memset(g_nb[:, :], NEG), 1)
        nb_done = halo0_done
        for Dc in range(DC):
            s_wd[Dc].inc(G.dma_start(
                winB[Dc], d_win[:, 3072 * Dc:3072 * (Dc + 1)]), 16)
        for Dc in range(DC):
            wG.wait(s_wd[Dc], 16)
        wout_done = s_gw.inc(G.dma_start(wout_region, d_wout), 16)
        wG.wait(s_gw, s_gw.n)
        b2_done = s_gw.inc(G.dma_start(b2B[:, :], d_b2), 16)
        wG.wait(s_gw, s_gw.n)
        bout_done = s_gw.inc(G.dma_start(boutB[:, :], d_bout), 16)

        bank_free = [(None, 0)] * 8
        bank_ready = [(None, 0)] * 8
        resid_mark = [0] * TOKC
        lnf_guard = [0]

        def t_wait(sem, val):
            wT.wait(sem, val)

        def bank_acquire(b):
            if bank_free[b][0] is not None:
                t_wait(*bank_free[b])

        def warmers(sem, upto):
            for d in (8, 4):
                v = upto - d
                if v > 0:
                    T.wait_ge(sem.h, v)
                    T.ldweights(selB[0:8, 0:128])

        def emit_ln_stats_batch(tokcs):
            nj = len(tokcs)
            wV.wait(s_s, preload_mark)
            sm = {}
            for j, tokc in enumerate(tokcs):
                rt = resid[:, 1024 * tokc:1024 * (tokc + 1)]
                if tokc < 4:
                    wV.wait(s_u0, ua)
                else:
                    wV.wait(s_u1, ub)
                wV.wait(s_v, resid_mark[tokc])
                s_v.inc(V.bn_stats(s_bn6[:, 12 * j:12 * j + 6], rt[:, 0:512]))
                sm[j] = s_v.inc(V.bn_stats(s_bn6[:, 12 * j + 6:12 * j + 12],
                                           rt[:, 512:1024]))
            am = {}
            for j in range(nj):
                wV.wait(s_v, sm[j])
                am[j] = s_v.inc(V.bn_aggr(s_bn[:, 2 * j:2 * j + 2],
                                          s_bn6[:, 12 * j:12 * j + 12]))
            vm = {}
            for j in range(nj):
                wV.wait(s_v, am[j])
                vm[j] = s_v.inc(V.tensor_scalar(
                    s_var[:, j:j + 1], s_bn[:, 2 * j + 1:2 * j + 2], EPS,
                    None, ALU.add))
            wS.wait(s_v, vm[nj - 1])
            lm = s_s.inc(S.activation(s_lnv[:, 0:nj], s_var[:, 0:nj], AF.Ln))
            wS.wait(s_s, lm)
            em = s_s.inc(S.activation(s_inv[:, 0:nj], s_lnv[:, 0:nj], AF.Exp,
                                      scale=-0.5))
            wV.wait(s_s, em)
            last = 0
            for j in range(nj):
                last = s_v.inc(V.scalar_tensor_tensor(
                    s_nmi[:, j:j + 1], s_bn[:, 2 * j:2 * j + 1], -1.0,
                    s_inv[:, j:j + 1], ALU.mult, ALU.mult))
            return last

        def emit_normalize(tokc, j, dest, vm):
            rt = resid[:, 1024 * tokc:1024 * (tokc + 1)]
            wV.wait(s_v, vm)
            wV.wait(s_t, lnf_guard[0])
            return s_v.inc(V.scalar_tensor_tensor(
                dest, rt, s_inv[:, j:j + 1],
                s_nmi[:, j:j + 1].broadcast_to([128, 1024]),
                ALU.mult, ALU.add))

        def emit_transpose_evac(src_f32, bb, dests_bf16, dests_f32, vmark,
                                evac_waits=()):
            t_wait(s_k, k_done)
            t_wait(s_v, vmark)
            for Dc in range(DC):
                b = bb + Dc // 4
                off = (Dc % 4) * 128
                if Dc % 4 == 0:
                    bank_acquire(b)
                ins = T.transpose(bank(b, 128, off),
                                  src_f32[:, 128 * Dc:128 * (Dc + 1)],
                                  identf[:, :])
                if Dc % 4 == 3:
                    bank_ready[b] = (s_t, s_t.inc(ins))
            lnf_guard[0] = s_t.n
            last_s = 0
            for sem, val in evac_waits:
                wS.wait(sem, val)
            for Dc in range(DC):
                b = bb + Dc // 4
                off = (Dc % 4) * 128
                wS.wait(*bank_ready[b])
                if dests_bf16 is not None:
                    last_s = s_s.inc(S.activation(dests_bf16[Dc],
                                                  bank(b, 128, off), AF.Copy))
                if dests_f32 is not None:
                    last_s = s_s.inc(S.activation(dests_f32[Dc],
                                                  bank(b, 128, off), AF.Copy))
            for b in (bb, bb + 1):
                bank_free[b] = (s_s, last_s)
            return last_s

        wgt_evac = [0] * 8
        wbc_mark = [0] * E

        def make_gate_half(X, ln2T_tguard=0, wbc_vguard_ref=(0,),
                           wgtT_tguard=0, region_tguard=0,
                           bufs=None, tfs=None):
            tokcs = [8 * X + j for j in range(8)]
            st = {}
            lnf_g = [region_tguard, region_tguard]
            dual_tf = tfs[0][0] is not tfs[1][0]
            gmm = {-1: 0, -2: 0}
            lg_mark = {}

            def stats_fn():
                st["bm"] = emit_ln_stats_batch(tokcs)

            def piece_fn(j):
                par = j % 2
                buf = bufs[par]
                ln2tf = tfs[par]
                wV.wait(s_v, st["bm"])
                wV.wait(s_t, lnf_g[par])
                nm = s_v.inc(V.scalar_tensor_tensor(
                    buf, resid[:, 1024 * tokcs[j]:1024 * (tokcs[j] + 1)],
                    s_inv[:, j:j + 1],
                    s_nmi[:, j:j + 1].broadcast_to([128, 1024]),
                    ALU.mult, ALU.add))
                tp = par * 2 if X == 1 else 0
                t_wait(s_k, k_done)
                warmers(s_v, nm)
                t_wait(s_v, nm)
                for Dc in range(DC):
                    b = tp + Dc // 4
                    off = (Dc % 4) * 128
                    if Dc % 4 == 0:
                        bank_acquire(b)
                    ins = T.transpose(bank(b, 128, off),
                                      buf[:, 128 * Dc:128 * (Dc + 1)],
                                      identf[:, :])
                    if Dc % 4 == 3:
                        bank_ready[b] = (s_t, s_t.inc(ins))
                lnf_g[par] = s_t.n
                wS.wait(s_t, ln2T_tguard)
                wS.wait(s_t, region_tguard)
                wS.wait(s_v, wbc_vguard_ref[0])
                wS.wait(s_t, gmm[j - 2] if dual_tf else gmm[j - 1])
                last_s = 0
                for Dc in range(DC):
                    b = tp + Dc // 4
                    off = (Dc % 4) * 128
                    wS.wait(*bank_ready[b])
                    s_s.inc(S.activation(ln2T[Dc][:, 128 * j:128 * (j + 1)],
                                         bank(b, 128, off), AF.Copy))
                    last_s = s_s.inc(S.activation(ln2tf[Dc],
                                                  bank(b, 128, off), AF.Copy))
                for b in (tp, tp + 1):
                    bank_free[b] = (s_s, last_s)
                if j == 3:
                    ln2t_half[X][0] = last_s
                if j == 7:
                    ln2t_half[X][1] = last_s
                bank_acquire(tp)
                wT.wait(s_s, last_s)
                for Dc in range(DC):
                    ins = T.matmul(bank(tp, 8), ln2tf[Dc],
                                   constsf[:, GW0 + 8 * Dc:GW0 + 8 * Dc + 8],
                                   start=(Dc == 0), stop=(Dc == DC - 1))
                gmm[j] = s_t.inc(ins)
                bank_ready[tp] = (s_t, gmm[j])
                wV.wait(s_t, gmm[j])
                wV.wait(s_gm, nb_done)
                lg_mark[j] = s_v.inc(V.tensor_tensor(
                    g_l8[:, 8 * j:8 * j + 8], bank(tp, 8),
                    constsf[:, GB0:GB0 + 8], ALU.add))
                bank_free[tp] = (s_v, lg_mark[j])

            def tail_fn():
                def ph(fn, waits):
                    marks = {}
                    for j in range(8):
                        wV.wait(s_v, waits[j])
                        marks[j] = fn(j)
                    return marks

                m = ph(lambda j: s_v.inc(V.reduce_max(
                    g_m1[:, j:j + 1], g_l8[:, 8 * j:8 * j + 8], AX)), lg_mark)
                m = ph(lambda j: s_v.inc(V.tensor_tensor(
                    g_e[:, 8 * j:8 * j + 8], g_l8[:, 8 * j:8 * j + 8],
                    g_m1[:, j:j + 1].broadcast_to([128, 8]), ALU.subtract)), m)
                wS.wait(s_v, m[7])
                em = s_s.inc(S.activation(g_e2[:, :], g_e[:, :], AF.Exp))
                wV.wait(s_s, em)
                m = ph(lambda j: s_v.inc(V.reduce_sum(
                    g_sum[:, j:j + 1], g_e2[:, 8 * j:8 * j + 8], AX)),
                    {j: 0 for j in range(8)})
                wV.wait(s_v, m[7])
                rm = s_v.inc(V.reciprocal(g_sum[:, :], g_sum[:, :]))
                m_pr = ph(lambda j: s_v.inc(V.tensor_tensor(
                    g_pr[:, 8 * j:8 * j + 8], g_e2[:, 8 * j:8 * j + 8],
                    g_sum[:, j:j + 1].broadcast_to([128, 8]), ALU.mult)),
                    {j: rm for j in range(8)})
                m = ph(lambda j: s_v.inc(V.scalar_tensor_tensor(
                    g_l2b[:, 8 * j:8 * j + 8], g_l8[:, 8 * j:8 * j + 8],
                    g_m1[:, j:j + 1], g_nb[:, :], ALU.is_ge, ALU.mult)),
                    {j: 0 for j in range(8)})
                m = ph(lambda j: s_v.inc(V.tensor_tensor(
                    g_l2[:, 8 * j:8 * j + 8], g_l8[:, 8 * j:8 * j + 8],
                    g_l2b[:, 8 * j:8 * j + 8], ALU.add)), m)
                m = ph(lambda j: s_v.inc(V.reduce_max(
                    g_m2[:, j:j + 1], g_l2[:, 8 * j:8 * j + 8], AX)), m)
                mw = {}
                for j in range(8):
                    wV.wait(s_v, m[j])
                    wV.wait(s_v, m_pr[j])
                    mw[j] = s_v.inc(V.scalar_tensor_tensor(
                        g_wgt[:, 8 * j:8 * j + 8], g_l8[:, 8 * j:8 * j + 8],
                        g_m2[:, j:j + 1], g_pr[:, 8 * j:8 * j + 8],
                        ALU.is_ge, ALU.mult))
                for j in range(8):
                    gb2 = (j % 2) * 2 + 1
                    bank_acquire(gb2)
                    t_wait(s_v, mw[j])
                    tm = s_t.inc(T.transpose(
                        psum[0:8, 512 * gb2:512 * gb2 + 128],
                        g_wgt[:, 8 * j:8 * j + 8], identf[:, :]))
                    bank_ready[gb2] = (s_t, tm)
                    wS.wait(s_t, tm)
                    wS.wait(s_t, wgtT_tguard)
                    sm2 = s_s.inc(S.activation(
                        wgtT[0:8, 128 * j:128 * (j + 1)],
                        psum[0:8, 512 * gb2:512 * gb2 + 128], AF.Copy))
                    bank_free[gb2] = (s_s, sm2)
                    wgt_evac[j] = sm2
                wgt_all = max(wgt_evac)
                for e in range(E):
                    b0 = (e % 2) * 2
                    bank_acquire(b0)
                    bank_acquire(b0 + 1)
                    t_wait(s_s, wgt_all)
                    wT.wait(s_k, k_done)
                    T.matmul(bank(b0), selB[:, 128 * e:128 * (e + 1)],
                             wgtT[0:8, 0:512], start=True, stop=True)
                    tm = s_t.inc(T.matmul(bank(b0 + 1),
                                          selB[:, 128 * e:128 * (e + 1)],
                                          wgtT[0:8, 512:1024], start=True,
                                          stop=True))
                    bank_ready[b0] = bank_ready[b0 + 1] = (s_t, tm)
                    wS.wait(s_t, tm)
                    if wbc_vguard_ref[0]:
                        wS.wait(s_v, wbc_vguard_ref[0])
                    sm2 = s_s.inc(S.activation(wbc[e][:, :], bank(b0, 1024),
                                               AF.Copy))
                    wbc_mark[e] = sm2
                    bank_free[b0] = bank_free[b0 + 1] = (s_s, sm2)
                ln2t_ready[X] = s_s.n

            return stats_fn, piece_fn, tail_fn

        ln2t_half = [[0, 0], [0, 0]]
        ln2t_ready = [0, 0]

        release = [0] * QC
        v2_rel = [0] * DC
        y_mark = [0] * DC
        wout_mark = [0]
        win_mark = [0]
        ln1T_ready = [0]
        evac_mark = {}
        wout_info = {}

        QORDER = []
        for c in range(DC):
            QORDER += [8 + c, 16 + c]
        QORDER += list(range(DC))

        def emit_ln1(t):
            last = 0
            bm = emit_ln_stats_batch([4 * t + j for j in range(4)])
            for j in range(4):
                nm = emit_normalize(4 * t + j, j, ln1f, bm)
                last = emit_transpose_evac(
                    ln1f, 6,
                    [ln1T[Dc][:, 128 * j:128 * (j + 1)] for Dc in range(DC)],
                    None, nm,
                    evac_waits=((s_t, win_mark[0]),))
            ln1T_ready[0] = last

        def emit_win(t):
            wT.wait(s_s, ln1T_ready[0])
            for i, qc in enumerate(QORDER):
                b = i % 2
                bank_acquire(b)
                warmers(s_v, release[qc])
                t_wait(s_v, release[qc])
                for Dc in range(DC):
                    wT.wait(s_wd[Dc], 16)
                    ins = T.matmul(bank(b),
                                   winB[Dc][:, 128 * qc:128 * (qc + 1)],
                                   ln1T[Dc][:, 0:512],
                                   start=(Dc == 0), stop=(Dc == DC - 1))
                bank_ready[b] = (s_t, s_t.inc(ins))
                wS.wait(*bank_ready[b])
                em = s_s.inc(S.activation(
                    qkvraw[qc][:, 8:520], bank(b), AF.Identity,
                    bias=constsf[:, BIN0 + qc:BIN0 + qc + 1]))
                evac_mark[qc] = em
                bank_free[b] = (s_s, em)
            win_mark[0] = s_t.n
            last_win_mark[0] = s_t.n

        def conv3(dst, rawap, qc, extra_waits=()):
            for w in extra_waits:
                wV.wait(s_v, w)
            wV.wait(s_s, evac_mark[qc])
            wV.wait(s_gm, halo0_done)
            m = s_v.inc(V.scalar_tensor_tensor(
                dst[:, 0:512], rawap[:, 6:518],
                constsf[:, SW0 + 3 * qc:SW0 + 3 * qc + 1],
                constsf[:, SB0 + qc:SB0 + qc + 1].broadcast_to([128, 512]),
                ALU.mult, ALU.add))
            for k in (1, 2):
                wV.wait(s_v, m)
                m = s_v.inc(V.scalar_tensor_tensor(
                    dst[:, 0:512], rawap[:, 6 + k:518 + k],
                    constsf[:, SW0 + 3 * qc + k:SW0 + 3 * qc + k + 1],
                    dst[:, 0:512], ALU.mult, ALU.add))
            return m

        def halo_copy(qc, skip=False):
            wV.wait(s_s, evac_mark[qc])
            if skip:
                return s_v.n
            return s_v.inc(V.tensor_copy(qkvraw[qc][:, 4:8],
                                         qkvraw[qc][:, 516:520]))

        def emit_conv(t, chunk_cb=None):
            last_tile = (t == ATILES - 1)
            def tap(dst, rawap, qc, k, prev):
                if prev:
                    wV.wait(s_v, prev)
                if k == 0:
                    wV.wait(s_s, evac_mark[qc])
                    wV.wait(s_gm, halo0_done)
                    return s_v.inc(V.scalar_tensor_tensor(
                        dst[:, 0:512], rawap[:, 6:518],
                        constsf[:, SW0 + 3 * qc:SW0 + 3 * qc + 1],
                        constsf[:, SB0 + qc:SB0 + qc + 1].broadcast_to(
                            [128, 512]),
                        ALU.mult, ALU.add))
                return s_v.inc(V.scalar_tensor_tensor(
                    dst[:, 0:512], rawap[:, 6 + k:518 + k],
                    constsf[:, SW0 + 3 * qc + k:SW0 + 3 * qc + k + 1],
                    dst[:, 0:512], ALU.mult, ALU.add))

            def itap(dst, c, k, prev):
                wV.wait(s_v, prev)
                if k == 0:
                    return s_v.inc(V.scalar_tensor_tensor(
                        dst[:, 0:512], v2[c][:, 5:517],
                        constsf[:, IW0 + 4 * c:IW0 + 4 * c + 1],
                        constsf[:, IB0 + c:IB0 + c + 1].broadcast_to(
                            [128, 512]),
                        ALU.mult, ALU.add))
                return s_v.inc(V.scalar_tensor_tensor(
                    dst[:, 0:512], v2[c][:, 5 + k:517 + k],
                    constsf[:, IW0 + 4 * c + k:IW0 + 4 * c + k + 1],
                    dst[:, 0:512], ALU.mult, ALU.add))

            temps = [tmpA, tmpB, tmpC]
            tfree = [0, 0, 0]
            for c in range(DC):
                ta = temps[(2 * c) % 3]
                tb = temps[(2 * c + 1) % 3]
                ma = tfree[(2 * c) % 3]
                mb = tfree[(2 * c + 1) % 3]
                for k in range(3):
                    ma = tap(ta, qkvraw[8 + c], 8 + c, k, ma)
                    mb = tap(tb, qkvraw[16 + c], 16 + c, k, mb)
                release[8 + c] = halo_copy(8 + c)
                release[16 + c] = halo_copy(16 + c)
                wV.wait(s_v, ma)
                wV.wait(s_v, mb)
                wV.wait(s_v, v2_rel[c])
                mv = s_v.inc(V.tensor_tensor(v2[c][:, 8:520], ta[:, 0:512],
                                             tb[:, 0:512], ALU.mult))
                mi = mq = mv
                for k in range(4):
                    mi = itap(tb, c, k, mi)
                    if k < 3:
                        mq = tap(ta, qkvraw[c], c, k, mq)
                release[c] = halo_copy(c)
                wV.wait(s_v, mi)
                v2_rel[c] = s_v.inc(V.tensor_copy(v2[c][:, 4:8],
                                                  v2[c][:, 516:520]))
                wV.wait(s_v, mq)
                wV.wait(s_t, wout_mark[0])
                ym = s_v.inc(V.tensor_tensor(ybuf[c][:, :], ta[:, 0:512],
                                             tb[:, 0:512], ALU.mult))
                y_mark[c] = ym
                tfree[(2 * c) % 3] = tfree[(2 * c + 1) % 3] = ym
                if chunk_cb is not None:
                    chunk_cb(c)

        def emit_wout_pair(t, js):
            wT.wait(s_gw, wout_done)
            wT.wait(s_gw, bout_done)
            wT.wait(s_k, k_done)
            for j in js:
                tokc = 4 * t + j
                bb = 2 + 2 * (j % 2)
                bank_acquire(bb)
                bank_acquire(bb + 1)
                T.matmul(bank(bb), onesB[:, :], boutB[:, 0:512],
                         start=True, stop=False)
                T.matmul(bank(bb + 1), onesB[:, :], boutB[:, 512:1024],
                         start=True, stop=False)
                for c in range(DC):
                    warmers(s_v, y_mark[c])
                    t_wait(s_v, y_mark[c])
                    T.matmul(bank(bb), ybuf[c][:, 128 * j:128 * (j + 1)],
                             woutB[c][:, 0:512], start=False,
                             stop=(c == DC - 1))
                    ins = T.matmul(bank(bb + 1),
                                   ybuf[c][:, 128 * j:128 * (j + 1)],
                                   woutB[c][:, 512:1024], start=False,
                                   stop=(c == DC - 1))
                gm = s_t.inc(ins)
                bank_ready[bb] = bank_ready[bb + 1] = (s_t, gm)
                wout_info[(t, j)] = (bb, gm, tokc)
            wout_mark[0] = s_t.n

        def emit_resid_pair(t, js):
            for j in js:
                bb, gm, tokc = wout_info.pop((t, j))
                rt = resid[:, 1024 * tokc:1024 * (tokc + 1)]
                wV.wait(s_t, gm)
                wV.wait(s_v, resid_mark[tokc])
                rm = s_v.inc(V.tensor_tensor(rt, rt, bank(bb, 1024), ALU.add))
                resid_mark[tokc] = rm
                bank_free[bb] = bank_free[bb + 1] = (s_v, rm)

        last_win_mark = [0]
        g0h = {}
        emit_ln1(0)
        emit_win(0)
        for t in range(ATILES):
            if t + 1 < ATILES:
                emit_ln1(t + 1)
            if t > 0:
                emit_resid_pair(t - 1, (0, 1))
                emit_wout_pair(t - 1, (2, 3))
            if t == ATILES - 1:
                g0h["fns"] = make_gate_half(
                    0, ln2T_tguard=0, region_tguard=last_win_mark[0],
                    bufs=(ln2ft, ln1f), tfs=(ln2tfA, ln2tfA))
                g0h["fns"][0]()
                for j in range(8):
                    g0h["fns"][1](j)
                emit_conv(t)
            else:
                emit_conv(t)
            if t > 0:
                emit_resid_pair(t - 1, (2, 3))
            if t + 1 < ATILES:
                emit_win(t + 1)
            emit_wout_pair(t, (0, 1))
        tl = ATILES - 1
        emit_resid_pair(tl, (0, 1))
        emit_wout_pair(tl, (2, 3))
        emit_resid_pair(tl, (2, 3))
        attn_v_done = s_v.n
        attn_t_done = s_t.n


        wloads = []
        for X in range(2):
            for e in range(E):
                wloads.append(("w1", X, e))
                wloads.append(("w2", X, e))
        w1_done, w2_done = {}, {}
        h_done_mark, o_done_mark = {}, {}

        def issue_wload(kind, X, e):
            if (X, e) == (0, 0):
                if kind == "w1":
                    wG.wait(s_t, last_win_mark[0])
                else:
                    wG.wait(s_t, attn_t_done)
                    wG.wait(s_v, attn_v_done)
            else:
                pe = (X, e - 1) if e > 0 else (X - 1, E - 1)
                if kind == "w1":
                    wG.wait(s_t, h_done_mark[pe])
                else:
                    wG.wait(s_t, o_done_mark[pe])
            wG.wait(s_gw, s_gw.n)
            if kind == "w1":
                w1_done[(X, e)] = s_gw.inc(G.dma_start(w1_region, d_w1[e]), 16)
            else:
                w2_done[(X, e)] = s_gw.inc(G.dma_start(w2_region, d_w2[e]), 16)

        issue_wload("w1", 0, 0)
        g0h["fns"][2]()

        issue_wload("w2", 0, 0)
        wl_state = [2]

        hs_mark = [[0] * HC, [0] * HC]
        hprev_scale = [[0] * HC, [0] * HC]

        def moe_expert(X, e, after_h_cb=None, o_cb=None, after_o_cb=None):
            wT.wait(s_gw, w1_done[(X, e)])
            for half, (lo, hi) in enumerate(((0, 512), (512, 1024))):
                wT.wait(s_s, ln2t_half[X][half])
                for Hcc in range(HC):
                    bb = (Hcc % 2) * 2 + half
                    bank_acquire(bb)
                    for Dc in range(DC):
                        lhs = w1B[Dc][:, 128 * Hcc:128 * (Hcc + 1)]
                        ins = T.matmul(bank(bb), lhs, ln2T[Dc][:, lo:hi],
                                       start=(Dc == 0), stop=(Dc == DC - 1))
                    gm = s_t.inc(ins)
                    bank_ready[bb] = (s_t, gm)
                    wS.wait(s_t, gm)
                    wS.wait(s_t, attn_t_done)
                    if hprev_scale[half][Hcc]:
                        wS.wait(s_v, hprev_scale[half][Hcc])
                    em = s_s.inc(S.activation(
                        hbuf[Hcc][:, lo:hi], bank(bb), AF.Relu,
                        bias=constsf[:, B10 + 16 * e + Hcc:
                                     B10 + 16 * e + Hcc + 1]))
                    bank_free[bb] = (s_s, em)
                    wV.wait(s_s, em)
                    wV.wait(s_s, wbc_mark[e])
                    hs_mark[half][Hcc] = s_v.inc(V.tensor_tensor(
                        hbuf[Hcc][:, lo:hi], hbuf[Hcc][:, lo:hi],
                        wbc[e][:, lo:hi], ALU.mult))
                    hprev_scale[half][Hcc] = hs_mark[half][Hcc]
            h_done_mark[(X, e)] = s_t.n
            if after_h_cb is not None:
                after_h_cb()
            if wl_state[0] < len(wloads):
                issue_wload(*wloads[wl_state[0]])
                wl_state[0] += 1
            wT.wait(s_gw, w2_done[(X, e)])
            for jj in range(8):
                tokc = 8 * X + jj
                bb = 4 + (jj % 2) * 2
                bank_acquire(bb)
                bank_acquire(bb + 1)
                if e == 0:
                    wT.wait(s_gw, b2_done)
                    wT.wait(s_s, wgt_evac[jj])
                    T.matmul(bank(bb), wgtT[0:8, 128 * jj:128 * (jj + 1)],
                             b2B[0:8, 0:512], start=True, stop=False)
                    T.matmul(bank(bb + 1), wgtT[0:8, 128 * jj:128 * (jj + 1)],
                             b2B[0:8, 512:1024], start=True, stop=False)
                for Hcc in range(HC):
                    t_wait(s_v, hs_mark[jj // 4][Hcc])
                    lhs = hbuf[Hcc][:, 128 * jj:128 * (jj + 1)]
                    st = (e != 0) and (Hcc == 0)
                    T.matmul(bank(bb), lhs, w2B[Hcc][:, 0:512],
                             start=st, stop=(Hcc == HC - 1))
                    ins = T.matmul(bank(bb + 1), lhs, w2B[Hcc][:, 512:1024],
                                   start=st, stop=(Hcc == HC - 1))
                gm = s_t.inc(ins)
                bank_ready[bb] = bank_ready[bb + 1] = (s_t, gm)
                rt = resid[:, 1024 * tokc:1024 * (tokc + 1)]
                wV.wait(s_t, gm)
                wV.wait(s_v, resid_mark[tokc])
                rm = s_v.inc(V.tensor_tensor(rt, rt, bank(bb, 1024), ALU.add))
                resid_mark[tokc] = rm
                bank_free[bb] = bank_free[bb + 1] = (s_v, rm)
                if e == E - 1:
                    wSy.wait(s_v, rm)
                    wSy.wait(s_o, s_o.n)
                    s_o.inc(sy.dma_start(
                        d_out[128 * tokc:128 * (tokc + 1), :], rt), 16)
                if o_cb is not None:
                    o_cb(jj)
            if after_o_cb is not None:
                after_o_cb()
            o_done_mark[(X, e)] = s_t.n
            if wl_state[0] < len(wloads):
                issue_wload(*wloads[wl_state[0]])
                wl_state[0] += 1

        for e in range(E - 2):
            moe_expert(0, e)
        g1_holder = {}

        g1_vguard = [0]

        def g1_stats_cb(jj):
            if jj == 0:
                g1_vguard[0] = max(max(hprev_scale[0]), max(hprev_scale[1]))
                g1_holder["fns"] = make_gate_half(
                    1, ln2T_tguard=0,
                    wbc_vguard_ref=g1_vguard,
                    wgtT_tguard=o_done_mark[(0, 0)],
                    bufs=(ln2ft, ln2ft2), tfs=(ln2tfA, ln2tfB))
                g1_holder["fns"][0]()

        moe_expert(0, E - 2, o_cb=g1_stats_cb)

        def g1_after_h():
            g1_vguard[0] = max(max(hprev_scale[0]), max(hprev_scale[1]))

        def g1_piece(jj):
            if jj >= 2:
                g1_holder["fns"][1](jj - 2)

        def g1_tail():
            g1_holder["fns"][1](6)
            g1_holder["fns"][1](7)
            g1_holder["fns"][2]()

        moe_expert(0, E - 1, after_h_cb=g1_after_h, o_cb=g1_piece,
                   after_o_cb=g1_tail)
        for e in range(E):
            moe_expert(1, e)

        wSy.wait(s_o, s_o.n)
        assert s_o.n == 16 * 16, s_o.n

    ctx.close()
    return nc


_PROG_CACHE = {}


def _get_prog():
    if "nc" not in _PROG_CACHE:
        _PROG_CACHE["nc"] = build_program()
    return _PROG_CACHE["nc"]


def kernel(L0_attn_u, L0_attn_W_in, L0_attn_b_in, L0_attn_short_w,
           L0_attn_short_b, L0_attn_inner_w, L0_attn_inner_b, L0_attn_W_out,
           L0_attn_b_out, L0_ffn_gate_w, L0_ffn_gate_b, L0_ffn_w1, L0_ffn_b1,
           L0_ffn_w2, L0_ffn_b2):
    f32 = np.float32
    u = np.ascontiguousarray(np.asarray(L0_attn_u, f32))
    W_in = np.asarray(L0_attn_W_in, f32)
    b_in = np.asarray(L0_attn_b_in, f32)
    short_w = np.asarray(L0_attn_short_w, f32)
    short_b = np.asarray(L0_attn_short_b, f32)
    inner_w = np.asarray(L0_attn_inner_w, f32)
    inner_b = np.asarray(L0_attn_inner_b, f32)
    W_out = np.asarray(L0_attn_W_out, f32)
    b_out = np.asarray(L0_attn_b_out, f32)
    gate_w = np.asarray(L0_ffn_gate_w, f32)
    gate_b = np.asarray(L0_ffn_gate_b, f32)
    w1 = np.asarray(L0_ffn_w1, f32)
    b1 = np.asarray(L0_ffn_b1, f32)
    w2 = np.asarray(L0_ffn_w2, f32)
    b2 = np.asarray(L0_ffn_b2, f32)

    winl = np.ascontiguousarray(
        W_in.reshape(8, 128, 3072).transpose(1, 0, 2).reshape(128, 24576))
    woutl = np.ascontiguousarray(
        W_out.reshape(8, 128, 1024).transpose(1, 0, 2).reshape(128, 8192))
    w1l = np.ascontiguousarray(
        w1.reshape(8, 8, 128, 2048).transpose(0, 2, 1, 3).reshape(8, 128, 16384))
    w2l = np.ascontiguousarray(
        w2.reshape(8, 16, 128, 1024).transpose(0, 2, 1, 3).reshape(8, 128, 16384))
    cf = np.zeros((128, NCONST + 128), f32)
    cf[:, SW0:SW0 + 72] = short_w.reshape(24, 128, 3).transpose(1, 0, 2).reshape(128, 72)
    cf[:, SB0:SB0 + 24] = short_b.reshape(24, 128).T
    cf[:, IW0:IW0 + 32] = inner_w.reshape(8, 128, 4).transpose(1, 0, 2).reshape(128, 32)
    cf[:, IB0:IB0 + 8] = inner_b.reshape(8, 128).T
    cf[:, BIN0:BIN0 + 24] = b_in.reshape(24, 128).T
    cf[:, B10:B10 + 128] = b1.reshape(8, 16, 128).transpose(2, 0, 1).reshape(128, 128)
    cf[:, GW0:GW0 + 64] = gate_w.reshape(8, 8, 128).transpose(2, 1, 0).reshape(128, 64)
    cf[:, GB0:GB0 + 8] = np.broadcast_to(gate_b, (128, 8))
    cf[:, NCONST:NCONST + 128] = np.eye(128, dtype=f32)
    b2r = np.ascontiguousarray(b2)
    boutr = np.ascontiguousarray(b_out.reshape(1, 1024))
    selb = np.zeros((8, 1152), ml_dtypes.bfloat16)
    for e in range(8):
        selb[e, 128 * e:128 * (e + 1)] = 1
    selb[0, 1024:1152] = 1

    shared = dict(winl=winl, woutl=woutl, w1l=w1l, w2l=w2l, constsf=cf,
                  b2r=b2r, boutr=boutr, selb=selb)
    in_maps = [dict(u=np.ascontiguousarray(u[c]), **shared)
               for c in range(NCORES)]

    nc = _get_prog()
    res = run_bass_kernel_spmd(nc, in_maps, core_ids=list(range(NCORES)))
    out = np.stack([res.results[c]["out"] for c in range(NCORES)], axis=0)
    return np.ascontiguousarray(out.astype(np.float32))


# version 26
# speedup vs baseline: 1.0095x; 1.0095x over previous
import numpy as np
import ml_dtypes
from contextlib import ExitStack

import concourse.bass as bass
import concourse.mybir as mybir
from concourse.bass_utils import run_bass_kernel_spmd

F32 = mybir.dt.float32
BF16 = mybir.dt.bfloat16
ALU = mybir.AluOpType
AF = mybir.ActivationFunctionType
AX = mybir.AxisListType.X

NCORES = 8
N = 2048
D = 1024
DC = 8
QC = 24
H = 2048
HC = 16
E = 8
TA = 512
ATILES = N // TA
TOKC = N // 128
HALF = N // 2
EPS = 1e-5
NEG = -1.0e30

SW0, SB0, IW0, IB0, BIN0, B10, GW0, GB0 = 0, 72, 96, 128, 136, 160, 288, 352
NCONST = 360

ARENA = 28672
A_HB = 0
A_W1 = 8192
A_W2 = 16384
A_WBC = 24576
A_LN2F = 24576
A_LN2T = 25600
A_WIN = 0
A_WOUT = 12288
A_QKV = 16384
A_V2 = 22624
A_TMPA = 24704
A_TMPB = 24964
A_Y = 25224
A_LN1T = 26248
A_LN1F = 27272


class Sem:
    def __init__(self, handle):
        self.h = handle
        self.n = 0

    def inc(self, ins, k=1):
        ins.then_inc(self.h, k)
        self.n += k
        return self.n


class Waiter:
    def __init__(self, eng):
        self.eng = eng
        self.seen = {}

    def wait(self, sem, val):
        if not val or val <= 0:
            return
        assert val <= sem.n, f"wait ahead of emission: {val} > {sem.n}"
        if self.seen.get(id(sem), -1) >= val:
            return
        self.eng.wait_ge(sem.h, val)
        self.seen[id(sem)] = val


def build_program():
    nc = bass.Bass("TRN2", target_bir_lowering=False,
                   dynamic_dma_scratch_size=8192)

    d_u = nc.dram_tensor("u", [N, D], F32, kind="ExternalInput").ap()
    d_out = nc.dram_tensor("out", [N, D], F32, kind="ExternalOutput").ap()
    d_win = nc.dram_tensor("winl", [128, DC * 3072], F32, kind="ExternalInput").ap()
    d_wout = nc.dram_tensor("woutl", [128, DC * 1024], F32, kind="ExternalInput").ap()
    d_w1 = nc.dram_tensor("w1l", [E, 128, DC * H], F32, kind="ExternalInput").ap()
    d_w2 = nc.dram_tensor("w2l", [E, 128, HC * 1024], F32, kind="ExternalInput").ap()
    d_cf = nc.dram_tensor("constsf", [128, NCONST + 128], F32, kind="ExternalInput").ap()
    d_b2 = nc.dram_tensor("b2r", [8, 1024], F32, kind="ExternalInput").ap()
    d_bout = nc.dram_tensor("boutr", [1, 1024], F32, kind="ExternalInput").ap()
    d_sel = nc.dram_tensor("selb", [8, 1152], BF16, kind="ExternalInput").ap()

    ctx = ExitStack()
    sb = lambda name, shape, dt: ctx.enter_context(nc.sbuf_tensor(name, shape, dt))

    resid = sb("resid", [128, TOKC * 1024], F32)
    ln2T_t = sb("ln2T", [128, DC * 1024], BF16)
    wgtT = sb("wgtT", [8, HALF], BF16)
    constsf_t = sb("constsf_s", [128, NCONST + 128], F32)
    constsf = constsf_t[:, 0:NCONST]
    identf = constsf_t[:, NCONST:NCONST + 128]
    b2B = sb("b2B", [8, 1024], BF16)
    boutB = sb("boutB", [1, 1024], BF16)
    selB_t = sb("selB", [8, 1152], BF16)
    selB = selB_t[:, 0:1024]
    onesB = selB_t[0:1, 1024:1152]
    g_l8 = sb("g_l8", [128, 64], F32)
    g_e = sb("g_e", [128, 64], F32)
    g_e2 = sb("g_e2", [128, 64], F32)
    g_pr = sb("g_pr", [128, 64], F32)
    g_l2 = sb("g_l2", [128, 64], F32)
    g_l2b = sb("g_l2b", [128, 64], F32)
    g_nb = sb("g_nb", [128, 8], F32)
    g_wgt = sb("g_wgt", [128, 64], F32)
    g_m1 = sb("g_m1", [128, 8], F32)
    g_m2 = sb("g_m2", [128, 8], F32)
    g_sum = sb("g_sum", [128, 8], F32)
    tmpC = sb("tmpC", [128, 512], F32)
    s_bn6 = sb("s_bn6", [128, 96], F32)
    s_bn = sb("s_bn", [128, 16], F32)
    s_var = sb("s_var", [128, 8], F32)
    s_lnv = sb("s_lnv", [128, 8], F32)
    s_inv = sb("s_inv", [128, 8], F32)
    s_nmi = sb("s_nmi", [128, 8], F32)

    psum = ctx.enter_context(nc.psum_tensor("psum", [128, 4096], F32))

    ARENA_BYTES = 123392
    ab = (nc.sbuf_top - ARENA_BYTES) // 32 * 32
    assert ab >= nc.sbuf_base, (ab, nc.sbuf_base, nc.sbuf_top)
    mt = lambda name, shape, dt, off: nc.alloc_sbuf_tensor_at(
        name, shape, dt, offset=ab + off).ap()
    w1bt = mt("w1bt", [128, 16384], BF16, 0)
    hbt = mt("hbt", [128, 16384], BF16, 32768)
    w2bt = mt("w2bt", [128, 16384], BF16, 65536)
    wbct = mt("wbct", [128, 8192], BF16, 98304)
    ln2ft = mt("ln2ft", [128, 1024], F32, 107008)
    ln2tft = mt("ln2tft", [128, 1024], F32, 111104)
    ln2ft2 = mt("ln2ft2", [128, 1024], F32, 115200)
    ln2tft2 = mt("ln2tft2", [128, 1024], F32, 119296)
    winbt = mt("winbt", [128, 24576], BF16, 0)
    woutbt = mt("woutbt", [128, 8192], BF16, 49152)
    qkvt = mt("qkvt", [128, 12480], BF16, 65536)
    v2t = mt("v2t", [128, 4160], BF16, 90496)
    tmpA = mt("tmpAt", [128, 512], F32, 98816)
    tmpB = mt("tmpBt", [128, 512], F32, 100864)
    ln1f = mt("ln1ft", [128, 1024], F32, 102912)
    ln1tt = mt("ln1tt", [128, 4096], BF16, 107008)
    ybt = mt("ybt", [128, 4096], BF16, 115200)

    def bank(i, w=512, off=0):
        return psum[:, 512 * i + off:512 * i + off + w]

    hbuf = [hbt[:, 1024 * i:1024 * (i + 1)] for i in range(HC)]
    w1B = [w1bt[:, 2048 * i:2048 * (i + 1)] for i in range(DC)]
    w2B = [w2bt[:, 1024 * i:1024 * (i + 1)] for i in range(HC)]
    wbc = [wbct[:, 1024 * i:1024 * (i + 1)] for i in range(E)]
    w1_region = w1bt
    w2_region = w2bt
    win_region = winbt
    wout_region = woutbt

    ln2tfA = [ln2tft[:, 128 * i:128 * (i + 1)] for i in range(DC)]
    ln2tfB = [ln2tft2[:, 128 * i:128 * (i + 1)] for i in range(DC)]
    ln2T = [ln2T_t[:, 1024 * i:1024 * (i + 1)] for i in range(DC)]
    winB = [winbt[:, 3072 * i:3072 * (i + 1)] for i in range(DC)]
    woutB = [woutbt[:, 1024 * i:1024 * (i + 1)] for i in range(DC)]
    qkvraw = [qkvt[:, 520 * i:520 * (i + 1)] for i in range(QC)]
    v2 = [v2t[:, 520 * i:520 * (i + 1)] for i in range(DC)]
    ybuf = [ybt[:, 512 * i:512 * (i + 1)] for i in range(DC)]
    ln1T = [ln1tt[:, 512 * i:512 * (i + 1)] for i in range(DC)]

    s_t = Sem(ctx.enter_context(nc.semaphore("s_t")))
    s_v = Sem(ctx.enter_context(nc.semaphore("s_v")))
    s_s = Sem(ctx.enter_context(nc.semaphore("s_s")))
    s_gw = Sem(ctx.enter_context(nc.semaphore("s_gw")))
    s_su = Sem(ctx.enter_context(nc.semaphore("s_su")))
    s_o = Sem(ctx.enter_context(nc.semaphore("s_o")))
    s_gm = Sem(ctx.enter_context(nc.semaphore("s_gm")))
    s_k = Sem(ctx.enter_context(nc.semaphore("s_k")))
    s_wd = [Sem(ctx.enter_context(nc.semaphore(f"s_wd{i}"))) for i in range(DC)]
    s_u0 = Sem(ctx.enter_context(nc.semaphore("s_u0")))
    s_u1 = Sem(ctx.enter_context(nc.semaphore("s_u1")))

    block = ctx.enter_context(nc.Block())

    @block.sync
    def _(sy):
        V, S, T, G = nc.vector, nc.scalar, nc.tensor, nc.gpsimd
        wV, wS, wT, wG, wSy = Waiter(V), Waiter(S), Waiter(T), Waiter(G), Waiter(sy)

        pre0 = s_gm.inc(G.memset(s_var[:, 0:1], 0.0), 1)
        ua = s_u0.inc(sy.dma_start(
            resid[:, 0:4096].rearrange("p (i d) -> p i d", d=1024),
            d_u.rearrange("(i p) d -> p i d", p=128)[:, 0:4]), 16)
        s_k.inc(sy.dma_start(constsf_t[:, :], d_cf), 16)
        s_k.inc(sy.dma_start(selB_t[:, :], d_sel), 16)
        k_done = s_k.n
        ub = s_u1.inc(sy.dma_start(
            resid[:, 4096:16384].rearrange("p (i d) -> p i d", d=1024),
            d_u.rearrange("(i p) d -> p i d", p=128)[:, 4:16]), 16)
        wS.wait(s_gm, pre0)
        preload_mark = s_s.inc(S.activation(s_lnv[:, 0:1], s_var[:, 0:1],
                                            AF.Exp))

        for qc in range(QC):
            s_gm.inc(G.memset(qkvraw[qc][:, 0:8], 0.0), 1)
        for c in range(DC):
            s_gm.inc(G.memset(v2[c][:, 0:8], 0.0), 1)
        halo0_done = s_gm.inc(G.memset(g_nb[:, :], NEG), 1)
        nb_done = halo0_done
        for Dc in range(DC):
            s_wd[Dc].inc(G.dma_start(
                winB[Dc], d_win[:, 3072 * Dc:3072 * (Dc + 1)]), 16)
        for Dc in range(DC):
            wG.wait(s_wd[Dc], 16)
        wout_done = s_gw.inc(G.dma_start(wout_region, d_wout), 16)
        wG.wait(s_gw, s_gw.n)
        b2_done = s_gw.inc(G.dma_start(b2B[:, :], d_b2), 16)
        wG.wait(s_gw, s_gw.n)
        bout_done = s_gw.inc(G.dma_start(boutB[:, :], d_bout), 16)

        bank_free = [(None, 0)] * 8
        bank_ready = [(None, 0)] * 8
        resid_mark = [0] * TOKC
        lnf_guard = [0]

        def t_wait(sem, val):
            wT.wait(sem, val)

        def bank_acquire(b):
            if bank_free[b][0] is not None:
                t_wait(*bank_free[b])

        def warmers(sem, upto):
            for d in (8, 4):
                v = upto - d
                if v > 0:
                    T.wait_ge(sem.h, v)
                    T.ldweights(selB[0:8, 0:128])

        def emit_ln_stats_batch(tokcs):
            nj = len(tokcs)
            wV.wait(s_s, preload_mark)
            sm = {}
            for j, tokc in enumerate(tokcs):
                rt = resid[:, 1024 * tokc:1024 * (tokc + 1)]
                if tokc < 4:
                    wV.wait(s_u0, ua)
                else:
                    wV.wait(s_u1, ub)
                wV.wait(s_v, resid_mark[tokc])
                s_v.inc(V.bn_stats(s_bn6[:, 12 * j:12 * j + 6], rt[:, 0:512]))
                sm[j] = s_v.inc(V.bn_stats(s_bn6[:, 12 * j + 6:12 * j + 12],
                                           rt[:, 512:1024]))
            am = {}
            for j in range(nj):
                wV.wait(s_v, sm[j])
                am[j] = s_v.inc(V.bn_aggr(s_bn[:, 2 * j:2 * j + 2],
                                          s_bn6[:, 12 * j:12 * j + 12]))
            vm = {}
            for j in range(nj):
                wV.wait(s_v, am[j])
                vm[j] = s_v.inc(V.tensor_scalar(
                    s_var[:, j:j + 1], s_bn[:, 2 * j + 1:2 * j + 2], EPS,
                    None, ALU.add))
            wS.wait(s_v, vm[nj - 1])
            lm = s_s.inc(S.activation(s_lnv[:, 0:nj], s_var[:, 0:nj], AF.Ln))
            wS.wait(s_s, lm)
            em = s_s.inc(S.activation(s_inv[:, 0:nj], s_lnv[:, 0:nj], AF.Exp,
                                      scale=-0.5))
            wV.wait(s_s, em)
            last = 0
            for j in range(nj):
                last = s_v.inc(V.scalar_tensor_tensor(
                    s_nmi[:, j:j + 1], s_bn[:, 2 * j:2 * j + 1], -1.0,
                    s_inv[:, j:j + 1], ALU.mult, ALU.mult))
            return last

        def emit_normalize(tokc, j, dest, vm):
            rt = resid[:, 1024 * tokc:1024 * (tokc + 1)]
            wV.wait(s_v, vm)
            wV.wait(s_t, lnf_guard[0])
            return s_v.inc(V.scalar_tensor_tensor(
                dest, rt, s_inv[:, j:j + 1],
                s_nmi[:, j:j + 1].broadcast_to([128, 1024]),
                ALU.mult, ALU.add))

        def emit_transpose_evac(src_f32, bb, dests_bf16, dests_f32, vmark,
                                evac_waits=()):
            t_wait(s_k, k_done)
            t_wait(s_v, vmark)
            for Dc in range(DC):
                b = bb + Dc // 4
                off = (Dc % 4) * 128
                if Dc % 4 == 0:
                    bank_acquire(b)
                ins = T.transpose(bank(b, 128, off),
                                  src_f32[:, 128 * Dc:128 * (Dc + 1)],
                                  identf[:, :])
                if Dc % 4 == 3:
                    bank_ready[b] = (s_t, s_t.inc(ins))
            lnf_guard[0] = s_t.n
            last_s = 0
            for sem, val in evac_waits:
                wS.wait(sem, val)
            for Dc in range(DC):
                b = bb + Dc // 4
                off = (Dc % 4) * 128
                wS.wait(*bank_ready[b])
                if dests_bf16 is not None:
                    last_s = s_s.inc(S.activation(dests_bf16[Dc],
                                                  bank(b, 128, off), AF.Copy))
                if dests_f32 is not None:
                    last_s = s_s.inc(S.activation(dests_f32[Dc],
                                                  bank(b, 128, off), AF.Copy))
            for b in (bb, bb + 1):
                bank_free[b] = (s_s, last_s)
            return last_s

        wgt_evac = [0] * 8
        wbc_mark = [0] * E

        def make_gate_half(X, ln2T_tguard=0, wbc_vguard_ref=(0,),
                           wgtT_tguard=0, region_tguard=0,
                           bufs=None, tfs=None):
            tokcs = [8 * X + j for j in range(8)]
            st = {}
            lnf_g = [region_tguard, region_tguard]
            dual_tf = tfs[0][0] is not tfs[1][0]
            gmm = {-1: 0, -2: 0}
            lg_mark = {}

            def stats_fn():
                st["bm"] = emit_ln_stats_batch(tokcs)

            def piece_fn(j):
                par = j % 2
                buf = bufs[par]
                ln2tf = tfs[par]
                wV.wait(s_v, st["bm"])
                wV.wait(s_t, lnf_g[par])
                nm = s_v.inc(V.scalar_tensor_tensor(
                    buf, resid[:, 1024 * tokcs[j]:1024 * (tokcs[j] + 1)],
                    s_inv[:, j:j + 1],
                    s_nmi[:, j:j + 1].broadcast_to([128, 1024]),
                    ALU.mult, ALU.add))
                tp = par * 2 if X == 1 else 0
                t_wait(s_k, k_done)
                warmers(s_v, nm)
                t_wait(s_v, nm)
                for Dc in range(DC):
                    b = tp + Dc // 4
                    off = (Dc % 4) * 128
                    if Dc % 4 == 0:
                        bank_acquire(b)
                    ins = T.transpose(bank(b, 128, off),
                                      buf[:, 128 * Dc:128 * (Dc + 1)],
                                      identf[:, :])
                    if Dc % 4 == 3:
                        bank_ready[b] = (s_t, s_t.inc(ins))
                lnf_g[par] = s_t.n
                wS.wait(s_t, ln2T_tguard)
                wS.wait(s_t, region_tguard)
                wS.wait(s_v, wbc_vguard_ref[0])
                wS.wait(s_t, gmm[j - 2] if dual_tf else gmm[j - 1])
                last_s = 0
                for Dc in range(DC):
                    b = tp + Dc // 4
                    off = (Dc % 4) * 128
                    wS.wait(*bank_ready[b])
                    s_s.inc(S.activation(ln2T[Dc][:, 128 * j:128 * (j + 1)],
                                         bank(b, 128, off), AF.Copy))
                    last_s = s_s.inc(S.activation(ln2tf[Dc],
                                                  bank(b, 128, off), AF.Copy))
                for b in (tp, tp + 1):
                    bank_free[b] = (s_s, last_s)
                if j == 3:
                    ln2t_half[X][0] = last_s
                if j == 7:
                    ln2t_half[X][1] = last_s
                bank_acquire(tp)
                wT.wait(s_s, last_s)
                for Dc in range(DC):
                    ins = T.matmul(bank(tp, 8), ln2tf[Dc],
                                   constsf[:, GW0 + 8 * Dc:GW0 + 8 * Dc + 8],
                                   start=(Dc == 0), stop=(Dc == DC - 1))
                gmm[j] = s_t.inc(ins)
                bank_ready[tp] = (s_t, gmm[j])
                wV.wait(s_t, gmm[j])
                wV.wait(s_gm, nb_done)
                lg_mark[j] = s_v.inc(V.tensor_tensor(
                    g_l8[:, 8 * j:8 * j + 8], bank(tp, 8),
                    constsf[:, GB0:GB0 + 8], ALU.add))
                bank_free[tp] = (s_v, lg_mark[j])

            def tail_fn(part=None):
                if part == "evac":
                    emit_tail_evac()
                    return
                def ph(fn, waits):
                    marks = {}
                    for j in range(8):
                        wV.wait(s_v, waits[j])
                        marks[j] = fn(j)
                    return marks

                m = ph(lambda j: s_v.inc(V.reduce_max(
                    g_m1[:, j:j + 1], g_l8[:, 8 * j:8 * j + 8], AX)), lg_mark)
                m = ph(lambda j: s_v.inc(V.tensor_tensor(
                    g_e[:, 8 * j:8 * j + 8], g_l8[:, 8 * j:8 * j + 8],
                    g_m1[:, j:j + 1].broadcast_to([128, 8]), ALU.subtract)), m)
                wS.wait(s_v, m[7])
                em = s_s.inc(S.activation(g_e2[:, :], g_e[:, :], AF.Exp))
                wV.wait(s_s, em)
                m = ph(lambda j: s_v.inc(V.reduce_sum(
                    g_sum[:, j:j + 1], g_e2[:, 8 * j:8 * j + 8], AX)),
                    {j: 0 for j in range(8)})
                wV.wait(s_v, m[7])
                rm = s_v.inc(V.reciprocal(g_sum[:, :], g_sum[:, :]))
                m_pr = ph(lambda j: s_v.inc(V.tensor_tensor(
                    g_pr[:, 8 * j:8 * j + 8], g_e2[:, 8 * j:8 * j + 8],
                    g_sum[:, j:j + 1].broadcast_to([128, 8]), ALU.mult)),
                    {j: rm for j in range(8)})
                m = ph(lambda j: s_v.inc(V.scalar_tensor_tensor(
                    g_l2b[:, 8 * j:8 * j + 8], g_l8[:, 8 * j:8 * j + 8],
                    g_m1[:, j:j + 1], g_nb[:, :], ALU.is_ge, ALU.mult)),
                    {j: 0 for j in range(8)})
                m = ph(lambda j: s_v.inc(V.tensor_tensor(
                    g_l2[:, 8 * j:8 * j + 8], g_l8[:, 8 * j:8 * j + 8],
                    g_l2b[:, 8 * j:8 * j + 8], ALU.add)), m)
                m = ph(lambda j: s_v.inc(V.reduce_max(
                    g_m2[:, j:j + 1], g_l2[:, 8 * j:8 * j + 8], AX)), m)
                mw = {}
                for j in range(8):
                    wV.wait(s_v, m[j])
                    wV.wait(s_v, m_pr[j])
                    mw[j] = s_v.inc(V.scalar_tensor_tensor(
                        g_wgt[:, 8 * j:8 * j + 8], g_l8[:, 8 * j:8 * j + 8],
                        g_m2[:, j:j + 1], g_pr[:, 8 * j:8 * j + 8],
                        ALU.is_ge, ALU.mult))
                for j in range(8):
                    gb2 = (j % 2) * 2 + 1
                    bank_acquire(gb2)
                    t_wait(s_v, mw[j])
                    tm = s_t.inc(T.transpose(
                        psum[0:8, 512 * gb2:512 * gb2 + 128],
                        g_wgt[:, 8 * j:8 * j + 8], identf[:, :]))
                    bank_ready[gb2] = (s_t, tm)
                    wS.wait(s_t, tm)
                    wS.wait(s_t, wgtT_tguard)
                    sm2 = s_s.inc(S.activation(
                        wgtT[0:8, 128 * j:128 * (j + 1)],
                        psum[0:8, 512 * gb2:512 * gb2 + 128], AF.Copy))
                    bank_free[gb2] = (s_s, sm2)
                    wgt_evac[j] = sm2
                if part == "head":
                    return
                emit_tail_evac()

            def emit_tail_evac():
                wgt_all = max(wgt_evac)
                for e in range(E):
                    b0 = (e % 2) * 2
                    bank_acquire(b0)
                    bank_acquire(b0 + 1)
                    t_wait(s_s, wgt_all)
                    wT.wait(s_k, k_done)
                    T.matmul(bank(b0), selB[:, 128 * e:128 * (e + 1)],
                             wgtT[0:8, 0:512], start=True, stop=True)
                    tm = s_t.inc(T.matmul(bank(b0 + 1),
                                          selB[:, 128 * e:128 * (e + 1)],
                                          wgtT[0:8, 512:1024], start=True,
                                          stop=True))
                    bank_ready[b0] = bank_ready[b0 + 1] = (s_t, tm)
                    wS.wait(s_t, tm)
                    if wbc_vguard_ref[0]:
                        wS.wait(s_v, wbc_vguard_ref[0])
                    sm2 = s_s.inc(S.activation(wbc[e][:, :], bank(b0, 1024),
                                               AF.Copy))
                    wbc_mark[e] = sm2
                    bank_free[b0] = bank_free[b0 + 1] = (s_s, sm2)
                ln2t_ready[X] = s_s.n

            return stats_fn, piece_fn, tail_fn

        ln2t_half = [[0, 0], [0, 0]]
        ln2t_ready = [0, 0]

        release = [0] * QC
        v2_rel = [0] * DC
        y_mark = [0] * DC
        wout_mark = [0]
        win_mark = [0]
        ln1T_ready = [0]
        evac_mark = {}
        wout_info = {}

        QORDER = []
        for c in range(DC):
            QORDER += [8 + c, 16 + c]
        QORDER += list(range(DC))

        def emit_ln1(t):
            last = 0
            bm = emit_ln_stats_batch([4 * t + j for j in range(4)])
            for j in range(4):
                nm = emit_normalize(4 * t + j, j, ln1f, bm)
                last = emit_transpose_evac(
                    ln1f, 6,
                    [ln1T[Dc][:, 128 * j:128 * (j + 1)] for Dc in range(DC)],
                    None, nm,
                    evac_waits=((s_t, win_mark[0]),))
            ln1T_ready[0] = last

        def emit_win(t):
            wT.wait(s_s, ln1T_ready[0])
            for i, qc in enumerate(QORDER):
                b = i % 2
                bank_acquire(b)
                warmers(s_v, release[qc])
                t_wait(s_v, release[qc])
                for Dc in range(DC):
                    wT.wait(s_wd[Dc], 16)
                    ins = T.matmul(bank(b),
                                   winB[Dc][:, 128 * qc:128 * (qc + 1)],
                                   ln1T[Dc][:, 0:512],
                                   start=(Dc == 0), stop=(Dc == DC - 1))
                bank_ready[b] = (s_t, s_t.inc(ins))
                wS.wait(*bank_ready[b])
                em = s_s.inc(S.activation(
                    qkvraw[qc][:, 8:520], bank(b), AF.Identity,
                    bias=constsf[:, BIN0 + qc:BIN0 + qc + 1]))
                evac_mark[qc] = em
                bank_free[b] = (s_s, em)
            win_mark[0] = s_t.n
            last_win_mark[0] = s_t.n

        def conv3(dst, rawap, qc, extra_waits=()):
            for w in extra_waits:
                wV.wait(s_v, w)
            wV.wait(s_s, evac_mark[qc])
            wV.wait(s_gm, halo0_done)
            m = s_v.inc(V.scalar_tensor_tensor(
                dst[:, 0:512], rawap[:, 6:518],
                constsf[:, SW0 + 3 * qc:SW0 + 3 * qc + 1],
                constsf[:, SB0 + qc:SB0 + qc + 1].broadcast_to([128, 512]),
                ALU.mult, ALU.add))
            for k in (1, 2):
                wV.wait(s_v, m)
                m = s_v.inc(V.scalar_tensor_tensor(
                    dst[:, 0:512], rawap[:, 6 + k:518 + k],
                    constsf[:, SW0 + 3 * qc + k:SW0 + 3 * qc + k + 1],
                    dst[:, 0:512], ALU.mult, ALU.add))
            return m

        def halo_copy(qc, skip=False):
            wV.wait(s_s, evac_mark[qc])
            if skip:
                return s_v.n
            return s_v.inc(V.tensor_copy(qkvraw[qc][:, 4:8],
                                         qkvraw[qc][:, 516:520]))

        def emit_conv(t, chunk_cb=None):
            last_tile = (t == ATILES - 1)
            def tap(dst, rawap, qc, k, prev):
                if prev:
                    wV.wait(s_v, prev)
                if k == 0:
                    wV.wait(s_s, evac_mark[qc])
                    wV.wait(s_gm, halo0_done)
                    return s_v.inc(V.scalar_tensor_tensor(
                        dst[:, 0:512], rawap[:, 6:518],
                        constsf[:, SW0 + 3 * qc:SW0 + 3 * qc + 1],
                        constsf[:, SB0 + qc:SB0 + qc + 1].broadcast_to(
                            [128, 512]),
                        ALU.mult, ALU.add))
                return s_v.inc(V.scalar_tensor_tensor(
                    dst[:, 0:512], rawap[:, 6 + k:518 + k],
                    constsf[:, SW0 + 3 * qc + k:SW0 + 3 * qc + k + 1],
                    dst[:, 0:512], ALU.mult, ALU.add))

            def itap(dst, c, k, prev):
                wV.wait(s_v, prev)
                if k == 0:
                    return s_v.inc(V.scalar_tensor_tensor(
                        dst[:, 0:512], v2[c][:, 5:517],
                        constsf[:, IW0 + 4 * c:IW0 + 4 * c + 1],
                        constsf[:, IB0 + c:IB0 + c + 1].broadcast_to(
                            [128, 512]),
                        ALU.mult, ALU.add))
                return s_v.inc(V.scalar_tensor_tensor(
                    dst[:, 0:512], v2[c][:, 5 + k:517 + k],
                    constsf[:, IW0 + 4 * c + k:IW0 + 4 * c + k + 1],
                    dst[:, 0:512], ALU.mult, ALU.add))

            temps = [tmpA, tmpB, tmpC]
            tfree = [0, 0, 0]
            for c in range(DC):
                ta = temps[(2 * c) % 3]
                tb = temps[(2 * c + 1) % 3]
                ma = tfree[(2 * c) % 3]
                mb = tfree[(2 * c + 1) % 3]
                for k in range(3):
                    ma = tap(ta, qkvraw[8 + c], 8 + c, k, ma)
                    mb = tap(tb, qkvraw[16 + c], 16 + c, k, mb)
                release[8 + c] = halo_copy(8 + c)
                release[16 + c] = halo_copy(16 + c)
                wV.wait(s_v, ma)
                wV.wait(s_v, mb)
                wV.wait(s_v, v2_rel[c])
                mv = s_v.inc(V.tensor_tensor(v2[c][:, 8:520], ta[:, 0:512],
                                             tb[:, 0:512], ALU.mult))
                mi = mq = mv
                for k in range(4):
                    mi = itap(tb, c, k, mi)
                    if k < 3:
                        mq = tap(ta, qkvraw[c], c, k, mq)
                release[c] = halo_copy(c)
                wV.wait(s_v, mi)
                v2_rel[c] = s_v.inc(V.tensor_copy(v2[c][:, 4:8],
                                                  v2[c][:, 516:520]))
                wV.wait(s_v, mq)
                wV.wait(s_t, wout_mark[0])
                ym = s_v.inc(V.tensor_tensor(ybuf[c][:, :], ta[:, 0:512],
                                             tb[:, 0:512], ALU.mult))
                y_mark[c] = ym
                tfree[(2 * c) % 3] = tfree[(2 * c + 1) % 3] = ym
                if chunk_cb is not None:
                    chunk_cb(c)

        def emit_wout_pair(t, js):
            wT.wait(s_gw, wout_done)
            wT.wait(s_gw, bout_done)
            wT.wait(s_k, k_done)
            for j in js:
                tokc = 4 * t + j
                bb = 2 + 2 * (j % 2)
                bank_acquire(bb)
                bank_acquire(bb + 1)
                T.matmul(bank(bb), onesB[:, :], boutB[:, 0:512],
                         start=True, stop=False)
                T.matmul(bank(bb + 1), onesB[:, :], boutB[:, 512:1024],
                         start=True, stop=False)
                for c in range(DC):
                    warmers(s_v, y_mark[c])
                    t_wait(s_v, y_mark[c])
                    T.matmul(bank(bb), ybuf[c][:, 128 * j:128 * (j + 1)],
                             woutB[c][:, 0:512], start=False,
                             stop=(c == DC - 1))
                    ins = T.matmul(bank(bb + 1),
                                   ybuf[c][:, 128 * j:128 * (j + 1)],
                                   woutB[c][:, 512:1024], start=False,
                                   stop=(c == DC - 1))
                gm = s_t.inc(ins)
                bank_ready[bb] = bank_ready[bb + 1] = (s_t, gm)
                wout_info[(t, j)] = (bb, gm, tokc)
            wout_mark[0] = s_t.n

        def emit_resid_pair(t, js):
            for j in js:
                bb, gm, tokc = wout_info.pop((t, j))
                rt = resid[:, 1024 * tokc:1024 * (tokc + 1)]
                wV.wait(s_t, gm)
                wV.wait(s_v, resid_mark[tokc])
                rm = s_v.inc(V.tensor_tensor(rt, rt, bank(bb, 1024), ALU.add))
                resid_mark[tokc] = rm
                bank_free[bb] = bank_free[bb + 1] = (s_v, rm)

        last_win_mark = [0]
        g0h = {}
        emit_ln1(0)
        emit_win(0)
        for t in range(ATILES):
            if t + 1 < ATILES:
                emit_ln1(t + 1)
            if t > 0:
                emit_resid_pair(t - 1, (0, 1))
                emit_wout_pair(t - 1, (2, 3))
            if t == ATILES - 1:
                g0h["fns"] = make_gate_half(
                    0, ln2T_tguard=0, region_tguard=last_win_mark[0],
                    bufs=(ln2ft, ln1f), tfs=(ln2tfA, ln2tfA))
                g0h["fns"][0]()
                for j in range(8):
                    g0h["fns"][1](j)
                g0h["fns"][2]("head")
                emit_conv(t)
            else:
                emit_conv(t)
            if t > 0:
                emit_resid_pair(t - 1, (2, 3))
            if t + 1 < ATILES:
                emit_win(t + 1)
            emit_wout_pair(t, (0, 1))
        tl = ATILES - 1
        emit_resid_pair(tl, (0, 1))
        emit_wout_pair(tl, (2, 3))
        emit_resid_pair(tl, (2, 3))
        attn_v_done = s_v.n
        attn_t_done = s_t.n


        wloads = []
        for X in range(2):
            for e in range(E):
                wloads.append(("w1", X, e))
                wloads.append(("w2", X, e))
        w1_done, w2_done = {}, {}
        h_done_mark, o_done_mark = {}, {}

        def issue_wload(kind, X, e):
            if (X, e) == (0, 0):
                if kind == "w1":
                    wG.wait(s_t, last_win_mark[0])
                else:
                    wG.wait(s_t, attn_t_done)
                    wG.wait(s_v, attn_v_done)
            else:
                pe = (X, e - 1) if e > 0 else (X - 1, E - 1)
                if kind == "w1":
                    wG.wait(s_t, h_done_mark[pe])
                else:
                    wG.wait(s_t, o_done_mark[pe])
            wG.wait(s_gw, s_gw.n)
            if kind == "w1":
                w1_done[(X, e)] = s_gw.inc(G.dma_start(w1_region, d_w1[e]), 16)
            else:
                w2_done[(X, e)] = s_gw.inc(G.dma_start(w2_region, d_w2[e]), 16)

        issue_wload("w1", 0, 0)
        g0h["fns"][2]("evac")

        issue_wload("w2", 0, 0)
        wl_state = [2]

        hs_mark = [[0] * HC, [0] * HC]
        hprev_scale = [[0] * HC, [0] * HC]

        def moe_expert(X, e, after_h_cb=None, o_cb=None, after_o_cb=None):
            wT.wait(s_gw, w1_done[(X, e)])
            for half, (lo, hi) in enumerate(((0, 512), (512, 1024))):
                wT.wait(s_s, ln2t_half[X][half])
                for Hcc in range(HC):
                    bb = (Hcc % 2) * 2 + half
                    bank_acquire(bb)
                    for Dc in range(DC):
                        lhs = w1B[Dc][:, 128 * Hcc:128 * (Hcc + 1)]
                        ins = T.matmul(bank(bb), lhs, ln2T[Dc][:, lo:hi],
                                       start=(Dc == 0), stop=(Dc == DC - 1))
                    gm = s_t.inc(ins)
                    bank_ready[bb] = (s_t, gm)
                    wS.wait(s_t, gm)
                    wS.wait(s_t, attn_t_done)
                    if hprev_scale[half][Hcc]:
                        wS.wait(s_v, hprev_scale[half][Hcc])
                    em = s_s.inc(S.activation(
                        hbuf[Hcc][:, lo:hi], bank(bb), AF.Relu,
                        bias=constsf[:, B10 + 16 * e + Hcc:
                                     B10 + 16 * e + Hcc + 1]))
                    bank_free[bb] = (s_s, em)
                    wV.wait(s_s, em)
                    wV.wait(s_s, wbc_mark[e])
                    hs_mark[half][Hcc] = s_v.inc(V.tensor_tensor(
                        hbuf[Hcc][:, lo:hi], hbuf[Hcc][:, lo:hi],
                        wbc[e][:, lo:hi], ALU.mult))
                    hprev_scale[half][Hcc] = hs_mark[half][Hcc]
            h_done_mark[(X, e)] = s_t.n
            if after_h_cb is not None:
                after_h_cb()
            if wl_state[0] < len(wloads):
                issue_wload(*wloads[wl_state[0]])
                wl_state[0] += 1
            wT.wait(s_gw, w2_done[(X, e)])
            for jj in range(8):
                tokc = 8 * X + jj
                bb = 4 + (jj % 2) * 2
                bank_acquire(bb)
                bank_acquire(bb + 1)
                if e == 0:
                    wT.wait(s_gw, b2_done)
                    wT.wait(s_s, wgt_evac[jj])
                    T.matmul(bank(bb), wgtT[0:8, 128 * jj:128 * (jj + 1)],
                             b2B[0:8, 0:512], start=True, stop=False)
                    T.matmul(bank(bb + 1), wgtT[0:8, 128 * jj:128 * (jj + 1)],
                             b2B[0:8, 512:1024], start=True, stop=False)
                for Hcc in range(HC):
                    t_wait(s_v, hs_mark[jj // 4][Hcc])
                    lhs = hbuf[Hcc][:, 128 * jj:128 * (jj + 1)]
                    st = (e != 0) and (Hcc == 0)
                    T.matmul(bank(bb), lhs, w2B[Hcc][:, 0:512],
                             start=st, stop=(Hcc == HC - 1))
                    ins = T.matmul(bank(bb + 1), lhs, w2B[Hcc][:, 512:1024],
                                   start=st, stop=(Hcc == HC - 1))
                gm = s_t.inc(ins)
                bank_ready[bb] = bank_ready[bb + 1] = (s_t, gm)
                rt = resid[:, 1024 * tokc:1024 * (tokc + 1)]
                wV.wait(s_t, gm)
                wV.wait(s_v, resid_mark[tokc])
                rm = s_v.inc(V.tensor_tensor(rt, rt, bank(bb, 1024), ALU.add))
                resid_mark[tokc] = rm
                bank_free[bb] = bank_free[bb + 1] = (s_v, rm)
                if e == E - 1:
                    wSy.wait(s_v, rm)
                    wSy.wait(s_o, s_o.n)
                    s_o.inc(sy.dma_start(
                        d_out[128 * tokc:128 * (tokc + 1), :], rt), 16)
                if o_cb is not None:
                    o_cb(jj)
            if after_o_cb is not None:
                after_o_cb()
            o_done_mark[(X, e)] = s_t.n
            if wl_state[0] < len(wloads):
                issue_wload(*wloads[wl_state[0]])
                wl_state[0] += 1

        for e in range(E - 2):
            moe_expert(0, e)
        g1_holder = {}

        g1_vguard = [0]

        def g1_stats_cb(jj):
            if jj == 0:
                g1_vguard[0] = max(max(hprev_scale[0]), max(hprev_scale[1]))
                g1_holder["fns"] = make_gate_half(
                    1, ln2T_tguard=0,
                    wbc_vguard_ref=g1_vguard,
                    wgtT_tguard=o_done_mark[(0, 0)],
                    bufs=(ln2ft, ln2ft2), tfs=(ln2tfA, ln2tfB))
                g1_holder["fns"][0]()

        moe_expert(0, E - 2, o_cb=g1_stats_cb)

        def g1_after_h():
            g1_vguard[0] = max(max(hprev_scale[0]), max(hprev_scale[1]))

        def g1_piece(jj):
            if jj >= 2:
                g1_holder["fns"][1](jj - 2)

        def g1_tail():
            g1_holder["fns"][1](6)
            g1_holder["fns"][1](7)
            g1_holder["fns"][2]()

        moe_expert(0, E - 1, after_h_cb=g1_after_h, o_cb=g1_piece,
                   after_o_cb=g1_tail)
        for e in range(E):
            moe_expert(1, e)

        wSy.wait(s_o, s_o.n)
        assert s_o.n == 16 * 16, s_o.n

    ctx.close()
    return nc


_PROG_CACHE = {}


def _get_prog():
    if "nc" not in _PROG_CACHE:
        _PROG_CACHE["nc"] = build_program()
    return _PROG_CACHE["nc"]


def kernel(L0_attn_u, L0_attn_W_in, L0_attn_b_in, L0_attn_short_w,
           L0_attn_short_b, L0_attn_inner_w, L0_attn_inner_b, L0_attn_W_out,
           L0_attn_b_out, L0_ffn_gate_w, L0_ffn_gate_b, L0_ffn_w1, L0_ffn_b1,
           L0_ffn_w2, L0_ffn_b2):
    f32 = np.float32
    u = np.ascontiguousarray(np.asarray(L0_attn_u, f32))
    W_in = np.asarray(L0_attn_W_in, f32)
    b_in = np.asarray(L0_attn_b_in, f32)
    short_w = np.asarray(L0_attn_short_w, f32)
    short_b = np.asarray(L0_attn_short_b, f32)
    inner_w = np.asarray(L0_attn_inner_w, f32)
    inner_b = np.asarray(L0_attn_inner_b, f32)
    W_out = np.asarray(L0_attn_W_out, f32)
    b_out = np.asarray(L0_attn_b_out, f32)
    gate_w = np.asarray(L0_ffn_gate_w, f32)
    gate_b = np.asarray(L0_ffn_gate_b, f32)
    w1 = np.asarray(L0_ffn_w1, f32)
    b1 = np.asarray(L0_ffn_b1, f32)
    w2 = np.asarray(L0_ffn_w2, f32)
    b2 = np.asarray(L0_ffn_b2, f32)

    winl = np.ascontiguousarray(
        W_in.reshape(8, 128, 3072).transpose(1, 0, 2).reshape(128, 24576))
    woutl = np.ascontiguousarray(
        W_out.reshape(8, 128, 1024).transpose(1, 0, 2).reshape(128, 8192))
    w1l = np.ascontiguousarray(
        w1.reshape(8, 8, 128, 2048).transpose(0, 2, 1, 3).reshape(8, 128, 16384))
    w2l = np.ascontiguousarray(
        w2.reshape(8, 16, 128, 1024).transpose(0, 2, 1, 3).reshape(8, 128, 16384))
    cf = np.zeros((128, NCONST + 128), f32)
    cf[:, SW0:SW0 + 72] = short_w.reshape(24, 128, 3).transpose(1, 0, 2).reshape(128, 72)
    cf[:, SB0:SB0 + 24] = short_b.reshape(24, 128).T
    cf[:, IW0:IW0 + 32] = inner_w.reshape(8, 128, 4).transpose(1, 0, 2).reshape(128, 32)
    cf[:, IB0:IB0 + 8] = inner_b.reshape(8, 128).T
    cf[:, BIN0:BIN0 + 24] = b_in.reshape(24, 128).T
    cf[:, B10:B10 + 128] = b1.reshape(8, 16, 128).transpose(2, 0, 1).reshape(128, 128)
    cf[:, GW0:GW0 + 64] = gate_w.reshape(8, 8, 128).transpose(2, 1, 0).reshape(128, 64)
    cf[:, GB0:GB0 + 8] = np.broadcast_to(gate_b, (128, 8))
    cf[:, NCONST:NCONST + 128] = np.eye(128, dtype=f32)
    b2r = np.ascontiguousarray(b2)
    boutr = np.ascontiguousarray(b_out.reshape(1, 1024))
    selb = np.zeros((8, 1152), ml_dtypes.bfloat16)
    for e in range(8):
        selb[e, 128 * e:128 * (e + 1)] = 1
    selb[0, 1024:1152] = 1

    shared = dict(winl=winl, woutl=woutl, w1l=w1l, w2l=w2l, constsf=cf,
                  b2r=b2r, boutr=boutr, selb=selb)
    in_maps = [dict(u=np.ascontiguousarray(u[c]), **shared)
               for c in range(NCORES)]

    nc = _get_prog()
    res = run_bass_kernel_spmd(nc, in_maps, core_ids=list(range(NCORES)))
    out = np.stack([res.results[c]["out"] for c in range(NCORES)], axis=0)
    return np.ascontiguousarray(out.astype(np.float32))


# version 27
# speedup vs baseline: 1.0134x; 1.0038x over previous
import numpy as np
import ml_dtypes
from contextlib import ExitStack

import concourse.bass as bass
import concourse.mybir as mybir
from concourse.bass_utils import run_bass_kernel_spmd

F32 = mybir.dt.float32
BF16 = mybir.dt.bfloat16
ALU = mybir.AluOpType
AF = mybir.ActivationFunctionType
AX = mybir.AxisListType.X

NCORES = 8
N = 2048
D = 1024
DC = 8
QC = 24
H = 2048
HC = 16
E = 8
TA = 512
ATILES = N // TA
TOKC = N // 128
HALF = N // 2
EPS = 1e-5
NEG = -1.0e30

SW0, SB0, IW0, IB0, BIN0, B10, GW0, GB0 = 0, 72, 96, 128, 136, 160, 288, 352
NCONST = 360

ARENA = 28672
A_HB = 0
A_W1 = 8192
A_W2 = 16384
A_WBC = 24576
A_LN2F = 24576
A_LN2T = 25600
A_WIN = 0
A_WOUT = 12288
A_QKV = 16384
A_V2 = 22624
A_TMPA = 24704
A_TMPB = 24964
A_Y = 25224
A_LN1T = 26248
A_LN1F = 27272


class Sem:
    def __init__(self, handle):
        self.h = handle
        self.n = 0

    def inc(self, ins, k=1):
        ins.then_inc(self.h, k)
        self.n += k
        return self.n


class Waiter:
    def __init__(self, eng):
        self.eng = eng
        self.seen = {}

    def wait(self, sem, val):
        if not val or val <= 0:
            return
        assert val <= sem.n, f"wait ahead of emission: {val} > {sem.n}"
        if self.seen.get(id(sem), -1) >= val:
            return
        self.eng.wait_ge(sem.h, val)
        self.seen[id(sem)] = val


def build_program():
    nc = bass.Bass("TRN2", target_bir_lowering=False,
                   dynamic_dma_scratch_size=8192)

    d_u = nc.dram_tensor("u", [N, D], F32, kind="ExternalInput").ap()
    d_out = nc.dram_tensor("out", [N, D], F32, kind="ExternalOutput").ap()
    d_win = nc.dram_tensor("winl", [128, DC * 3072], F32, kind="ExternalInput").ap()
    d_wout = nc.dram_tensor("woutl", [128, DC * 1024], F32, kind="ExternalInput").ap()
    d_w1 = nc.dram_tensor("w1l", [E, 128, DC * H], F32, kind="ExternalInput").ap()
    d_w2 = nc.dram_tensor("w2l", [E, 128, HC * 1024], F32, kind="ExternalInput").ap()
    d_cf = nc.dram_tensor("constsf", [128, NCONST + 128], F32, kind="ExternalInput").ap()
    d_b2 = nc.dram_tensor("b2r", [8, 1024], F32, kind="ExternalInput").ap()
    d_bout = nc.dram_tensor("boutr", [1, 1024], F32, kind="ExternalInput").ap()
    d_sel = nc.dram_tensor("selb", [8, 1152], BF16, kind="ExternalInput").ap()

    ctx = ExitStack()
    sb = lambda name, shape, dt: ctx.enter_context(nc.sbuf_tensor(name, shape, dt))

    resid = sb("resid", [128, TOKC * 1024], F32)
    ln2T_t = sb("ln2T", [128, DC * 1024], BF16)
    wgtT = sb("wgtT", [8, HALF], BF16)
    constsf_t = sb("constsf_s", [128, NCONST + 128], F32)
    constsf = constsf_t[:, 0:NCONST]
    identf = constsf_t[:, NCONST:NCONST + 128]
    b2B = sb("b2B", [8, 1024], BF16)
    boutB = sb("boutB", [1, 1024], BF16)
    selB_t = sb("selB", [8, 1152], BF16)
    selB = selB_t[:, 0:1024]
    onesB = selB_t[0:1, 1024:1152]
    g_l8 = sb("g_l8", [128, 64], F32)
    g_e = sb("g_e", [128, 64], F32)
    g_e2 = sb("g_e2", [128, 64], F32)
    g_pr = sb("g_pr", [128, 64], F32)
    g_l2 = sb("g_l2", [128, 64], F32)
    g_l2b = sb("g_l2b", [128, 64], F32)
    g_nb = sb("g_nb", [128, 8], F32)
    g_wgt = sb("g_wgt", [128, 64], F32)
    g_m1 = sb("g_m1", [128, 8], F32)
    g_m2 = sb("g_m2", [128, 8], F32)
    g_sum = sb("g_sum", [128, 8], F32)
    tmpC = sb("tmpC", [128, 512], F32)
    s_bn6 = sb("s_bn6", [128, 96], F32)
    s_bn = sb("s_bn", [128, 16], F32)
    s_var = sb("s_var", [128, 8], F32)
    s_lnv = sb("s_lnv", [128, 8], F32)
    s_inv = sb("s_inv", [128, 8], F32)
    s_nmi = sb("s_nmi", [128, 8], F32)

    psum = ctx.enter_context(nc.psum_tensor("psum", [128, 4096], F32))

    ARENA_BYTES = 123392
    ab = (nc.sbuf_top - ARENA_BYTES) // 32 * 32
    assert ab >= nc.sbuf_base, (ab, nc.sbuf_base, nc.sbuf_top)
    mt = lambda name, shape, dt, off: nc.alloc_sbuf_tensor_at(
        name, shape, dt, offset=ab + off).ap()
    w1bt = mt("w1bt", [128, 16384], BF16, 0)
    hbt = mt("hbt", [128, 16384], BF16, 32768)
    w2bt = mt("w2bt", [128, 16384], BF16, 65536)
    wbct = mt("wbct", [128, 8192], BF16, 98304)
    ln2ft = mt("ln2ft", [128, 1024], F32, 107008)
    ln2tft = mt("ln2tft", [128, 1024], F32, 111104)
    ln2ft2 = mt("ln2ft2", [128, 1024], F32, 115200)
    ln2tft2 = mt("ln2tft2", [128, 1024], F32, 119296)
    winbt = mt("winbt", [128, 24576], BF16, 0)
    woutbt = mt("woutbt", [128, 8192], BF16, 49152)
    qkvt = mt("qkvt", [128, 12480], BF16, 65536)
    v2t = mt("v2t", [128, 4160], BF16, 90496)
    tmpA = mt("tmpAt", [128, 512], F32, 98816)
    tmpB = mt("tmpBt", [128, 512], F32, 100864)
    ln1f = mt("ln1ft", [128, 1024], F32, 102912)
    ln1tt = mt("ln1tt", [128, 4096], BF16, 107008)
    ybt = mt("ybt", [128, 4096], BF16, 115200)

    def bank(i, w=512, off=0):
        return psum[:, 512 * i + off:512 * i + off + w]

    hbuf = [hbt[:, 1024 * i:1024 * (i + 1)] for i in range(HC)]
    w1B = [w1bt[:, 2048 * i:2048 * (i + 1)] for i in range(DC)]
    w2B = [w2bt[:, 1024 * i:1024 * (i + 1)] for i in range(HC)]
    wbc = [wbct[:, 1024 * i:1024 * (i + 1)] for i in range(E)]
    w1_region = w1bt
    w2_region = w2bt
    win_region = winbt
    wout_region = woutbt

    ln2tfA = [ln2tft[:, 128 * i:128 * (i + 1)] for i in range(DC)]
    ln2tfB = [ln2tft2[:, 128 * i:128 * (i + 1)] for i in range(DC)]
    ln2T = [ln2T_t[:, 1024 * i:1024 * (i + 1)] for i in range(DC)]
    winB = [winbt[:, 3072 * i:3072 * (i + 1)] for i in range(DC)]
    woutB = [woutbt[:, 1024 * i:1024 * (i + 1)] for i in range(DC)]
    qkvraw = [qkvt[:, 520 * i:520 * (i + 1)] for i in range(QC)]
    v2 = [v2t[:, 520 * i:520 * (i + 1)] for i in range(DC)]
    ybuf = [ybt[:, 512 * i:512 * (i + 1)] for i in range(DC)]
    ln1T = [ln1tt[:, 512 * i:512 * (i + 1)] for i in range(DC)]

    s_t = Sem(ctx.enter_context(nc.semaphore("s_t")))
    s_v = Sem(ctx.enter_context(nc.semaphore("s_v")))
    s_s = Sem(ctx.enter_context(nc.semaphore("s_s")))
    s_gw = Sem(ctx.enter_context(nc.semaphore("s_gw")))
    s_su = Sem(ctx.enter_context(nc.semaphore("s_su")))
    s_o = Sem(ctx.enter_context(nc.semaphore("s_o")))
    s_gm = Sem(ctx.enter_context(nc.semaphore("s_gm")))
    s_k = Sem(ctx.enter_context(nc.semaphore("s_k")))
    s_wd = [Sem(ctx.enter_context(nc.semaphore(f"s_wd{i}"))) for i in range(DC)]
    s_u0 = Sem(ctx.enter_context(nc.semaphore("s_u0")))
    s_u1 = Sem(ctx.enter_context(nc.semaphore("s_u1")))

    block = ctx.enter_context(nc.Block())

    @block.sync
    def _(sy):
        V, S, T, G = nc.vector, nc.scalar, nc.tensor, nc.gpsimd
        wV, wS, wT, wG, wSy = Waiter(V), Waiter(S), Waiter(T), Waiter(G), Waiter(sy)

        pre0 = s_gm.inc(G.memset(s_var[:, 0:1], 0.0), 1)
        ua = s_u0.inc(sy.dma_start(
            resid[:, 0:4096].rearrange("p (i d) -> p i d", d=1024),
            d_u.rearrange("(i p) d -> p i d", p=128)[:, 0:4]), 16)
        s_k.inc(sy.dma_start(constsf_t[:, :], d_cf), 16)
        s_k.inc(sy.dma_start(selB_t[:, :], d_sel), 16)
        k_done = s_k.n
        ub = s_u1.inc(sy.dma_start(
            resid[:, 4096:16384].rearrange("p (i d) -> p i d", d=1024),
            d_u.rearrange("(i p) d -> p i d", p=128)[:, 4:16]), 16)
        wS.wait(s_gm, pre0)
        preload_mark = s_s.inc(S.activation(s_lnv[:, 0:1], s_var[:, 0:1],
                                            AF.Exp))

        for qc in range(QC):
            s_gm.inc(G.memset(qkvraw[qc][:, 0:8], 0.0), 1)
        for c in range(DC):
            s_gm.inc(G.memset(v2[c][:, 0:8], 0.0), 1)
        halo0_done = s_gm.inc(G.memset(g_nb[:, :], NEG), 1)
        nb_done = halo0_done
        for Dc in range(DC):
            s_wd[Dc].inc(G.dma_start(
                winB[Dc], d_win[:, 3072 * Dc:3072 * (Dc + 1)]), 16)
        for Dc in range(DC):
            wG.wait(s_wd[Dc], 16)
        wout_done = s_gw.inc(G.dma_start(wout_region, d_wout), 16)
        wG.wait(s_gw, s_gw.n)
        b2_done = s_gw.inc(G.dma_start(b2B[:, :], d_b2), 16)
        wG.wait(s_gw, s_gw.n)
        bout_done = s_gw.inc(G.dma_start(boutB[:, :], d_bout), 16)

        bank_free = [(None, 0)] * 8
        bank_ready = [(None, 0)] * 8
        resid_mark = [0] * TOKC
        lnf_guard = [0]

        def t_wait(sem, val):
            wT.wait(sem, val)

        def bank_acquire(b):
            if bank_free[b][0] is not None:
                t_wait(*bank_free[b])

        def warmers(sem, upto):
            for d in (8, 4):
                v = upto - d
                if v > 0:
                    T.wait_ge(sem.h, v)
                    T.ldweights(selB[0:8, 0:128])

        def emit_ln_stats_batch(tokcs):
            nj = len(tokcs)
            wV.wait(s_s, preload_mark)
            sm = {}
            for j, tokc in enumerate(tokcs):
                rt = resid[:, 1024 * tokc:1024 * (tokc + 1)]
                if tokc < 4:
                    wV.wait(s_u0, ua)
                else:
                    wV.wait(s_u1, ub)
                wV.wait(s_v, resid_mark[tokc])
                s_v.inc(V.bn_stats(s_bn6[:, 12 * j:12 * j + 6], rt[:, 0:512]))
                sm[j] = s_v.inc(V.bn_stats(s_bn6[:, 12 * j + 6:12 * j + 12],
                                           rt[:, 512:1024]))
            am = {}
            for j in range(nj):
                wV.wait(s_v, sm[j])
                am[j] = s_v.inc(V.bn_aggr(s_bn[:, 2 * j:2 * j + 2],
                                          s_bn6[:, 12 * j:12 * j + 12]))
            vm = {}
            for j in range(nj):
                wV.wait(s_v, am[j])
                vm[j] = s_v.inc(V.tensor_scalar(
                    s_var[:, j:j + 1], s_bn[:, 2 * j + 1:2 * j + 2], EPS,
                    None, ALU.add))
            wS.wait(s_v, vm[nj - 1])
            lm = s_s.inc(S.activation(s_lnv[:, 0:nj], s_var[:, 0:nj], AF.Ln))
            wS.wait(s_s, lm)
            em = s_s.inc(S.activation(s_inv[:, 0:nj], s_lnv[:, 0:nj], AF.Exp,
                                      scale=-0.5))
            wV.wait(s_s, em)
            last = 0
            for j in range(nj):
                last = s_v.inc(V.scalar_tensor_tensor(
                    s_nmi[:, j:j + 1], s_bn[:, 2 * j:2 * j + 1], -1.0,
                    s_inv[:, j:j + 1], ALU.mult, ALU.mult))
            return last

        def emit_normalize(tokc, j, dest, vm):
            rt = resid[:, 1024 * tokc:1024 * (tokc + 1)]
            wV.wait(s_v, vm)
            wV.wait(s_t, lnf_guard[0])
            return s_v.inc(V.scalar_tensor_tensor(
                dest, rt, s_inv[:, j:j + 1],
                s_nmi[:, j:j + 1].broadcast_to([128, 1024]),
                ALU.mult, ALU.add))

        def emit_transpose_evac(src_f32, bb, dests_bf16, dests_f32, vmark,
                                evac_waits=()):
            t_wait(s_k, k_done)
            t_wait(s_v, vmark)
            for Dc in range(DC):
                b = bb + Dc // 4
                off = (Dc % 4) * 128
                if Dc % 4 == 0:
                    bank_acquire(b)
                ins = T.transpose(bank(b, 128, off),
                                  src_f32[:, 128 * Dc:128 * (Dc + 1)],
                                  identf[:, :])
                if Dc % 4 == 3:
                    bank_ready[b] = (s_t, s_t.inc(ins))
            lnf_guard[0] = s_t.n
            last_s = 0
            for sem, val in evac_waits:
                wS.wait(sem, val)
            for Dc in range(DC):
                b = bb + Dc // 4
                off = (Dc % 4) * 128
                wS.wait(*bank_ready[b])
                if dests_bf16 is not None:
                    last_s = s_s.inc(S.activation(dests_bf16[Dc],
                                                  bank(b, 128, off), AF.Copy))
                if dests_f32 is not None:
                    last_s = s_s.inc(S.activation(dests_f32[Dc],
                                                  bank(b, 128, off), AF.Copy))
            for b in (bb, bb + 1):
                bank_free[b] = (s_s, last_s)
            return last_s

        wgt_evac = [0] * 8
        wbc_mark = [0] * E

        def make_gate_half(X, ln2T_tguard=0, wbc_vguard_ref=(0,),
                           wgtT_tguard=0, region_tguard=0,
                           bufs=None, tfs=None):
            tokcs = [8 * X + j for j in range(8)]
            st = {}
            lnf_g = [region_tguard, region_tguard]
            dual_tf = tfs[0][0] is not tfs[1][0]
            gmm = {-1: 0, -2: 0}
            lg_mark = {}

            def stats_fn():
                st["bm"] = emit_ln_stats_batch(tokcs)

            def piece_fn(j):
                par = j % 2
                buf = bufs[par]
                ln2tf = tfs[par]
                wV.wait(s_v, st["bm"])
                wV.wait(s_t, lnf_g[par])
                nm = s_v.inc(V.scalar_tensor_tensor(
                    buf, resid[:, 1024 * tokcs[j]:1024 * (tokcs[j] + 1)],
                    s_inv[:, j:j + 1],
                    s_nmi[:, j:j + 1].broadcast_to([128, 1024]),
                    ALU.mult, ALU.add))
                tp = par * 2 if X == 1 else 0
                t_wait(s_k, k_done)
                warmers(s_v, nm)
                t_wait(s_v, nm)
                for Dc in range(DC):
                    b = tp + Dc // 4
                    off = (Dc % 4) * 128
                    if Dc % 4 == 0:
                        bank_acquire(b)
                    ins = T.transpose(bank(b, 128, off),
                                      buf[:, 128 * Dc:128 * (Dc + 1)],
                                      identf[:, :])
                    if Dc % 4 == 3:
                        bank_ready[b] = (s_t, s_t.inc(ins))
                lnf_g[par] = s_t.n
                wS.wait(s_t, ln2T_tguard)
                wS.wait(s_t, region_tguard)
                wS.wait(s_v, wbc_vguard_ref[0])
                wS.wait(s_t, gmm[j - 2] if dual_tf else gmm[j - 1])
                last_s = 0
                for Dc in range(DC):
                    b = tp + Dc // 4
                    off = (Dc % 4) * 128
                    wS.wait(*bank_ready[b])
                    s_s.inc(S.activation(ln2T[Dc][:, 128 * j:128 * (j + 1)],
                                         bank(b, 128, off), AF.Copy))
                    last_s = s_s.inc(S.activation(ln2tf[Dc],
                                                  bank(b, 128, off), AF.Copy))
                for b in (tp, tp + 1):
                    bank_free[b] = (s_s, last_s)
                if j == 3:
                    ln2t_half[X][0] = last_s
                if j == 7:
                    ln2t_half[X][1] = last_s
                bank_acquire(tp)
                wT.wait(s_s, last_s)
                for Dc in range(DC):
                    ins = T.matmul(bank(tp, 8), ln2tf[Dc],
                                   constsf[:, GW0 + 8 * Dc:GW0 + 8 * Dc + 8],
                                   start=(Dc == 0), stop=(Dc == DC - 1))
                gmm[j] = s_t.inc(ins)
                bank_ready[tp] = (s_t, gmm[j])
                wV.wait(s_t, gmm[j])
                wV.wait(s_gm, nb_done)
                lg_mark[j] = s_v.inc(V.tensor_tensor(
                    g_l8[:, 8 * j:8 * j + 8], bank(tp, 8),
                    constsf[:, GB0:GB0 + 8], ALU.add))
                bank_free[tp] = (s_v, lg_mark[j])

            def tail_fn(part=None):
                if part == "evac":
                    emit_tail_evac()
                    return
                def ph(fn, waits):
                    marks = {}
                    for j in range(8):
                        wV.wait(s_v, waits[j])
                        marks[j] = fn(j)
                    return marks

                m = ph(lambda j: s_v.inc(V.reduce_max(
                    g_m1[:, j:j + 1], g_l8[:, 8 * j:8 * j + 8], AX)), lg_mark)
                m_sub = ph(lambda j: s_v.inc(V.tensor_tensor(
                    g_e[:, 8 * j:8 * j + 8], g_l8[:, 8 * j:8 * j + 8],
                    g_m1[:, j:j + 1].broadcast_to([128, 8]), ALU.subtract)), m)
                m = m_sub
                wS.wait(s_v, m[7])
                em = s_s.inc(S.activation(g_e2[:, :], g_e[:, :], AF.Exp))
                wV.wait(s_s, em)
                m = ph(lambda j: s_v.inc(V.reduce_sum(
                    g_sum[:, j:j + 1], g_e2[:, 8 * j:8 * j + 8], AX)),
                    {j: 0 for j in range(8)})
                wV.wait(s_v, m[7])
                rm = s_v.inc(V.reciprocal(g_sum[:, :], g_sum[:, :]))
                m_pr = ph(lambda j: s_v.inc(V.tensor_tensor(
                    g_pr[:, 8 * j:8 * j + 8], g_e2[:, 8 * j:8 * j + 8],
                    g_sum[:, j:j + 1].broadcast_to([128, 8]), ALU.mult)),
                    {j: rm for j in range(8)})
                m = ph(lambda j: s_v.inc(V.scalar_tensor_tensor(
                    g_l2b[:, 8 * j:8 * j + 8], g_l8[:, 8 * j:8 * j + 8],
                    g_m1[:, j:j + 1], g_nb[:, :], ALU.is_ge, ALU.mult)),
                    m_sub)
                m = ph(lambda j: s_v.inc(V.tensor_tensor(
                    g_l2[:, 8 * j:8 * j + 8], g_l8[:, 8 * j:8 * j + 8],
                    g_l2b[:, 8 * j:8 * j + 8], ALU.add)), m)
                m = ph(lambda j: s_v.inc(V.reduce_max(
                    g_m2[:, j:j + 1], g_l2[:, 8 * j:8 * j + 8], AX)), m)
                mw = {}
                for j in range(8):
                    wV.wait(s_v, m[j])
                    wV.wait(s_v, m_pr[j])
                    mw[j] = s_v.inc(V.scalar_tensor_tensor(
                        g_wgt[:, 8 * j:8 * j + 8], g_l8[:, 8 * j:8 * j + 8],
                        g_m2[:, j:j + 1], g_pr[:, 8 * j:8 * j + 8],
                        ALU.is_ge, ALU.mult))
                for j in range(8):
                    gb2 = (j % 2) * 2 + 1
                    bank_acquire(gb2)
                    t_wait(s_v, mw[j])
                    tm = s_t.inc(T.transpose(
                        psum[0:8, 512 * gb2:512 * gb2 + 128],
                        g_wgt[:, 8 * j:8 * j + 8], identf[:, :]))
                    bank_ready[gb2] = (s_t, tm)
                    wS.wait(s_t, tm)
                    wS.wait(s_t, wgtT_tguard)
                    sm2 = s_s.inc(S.activation(
                        wgtT[0:8, 128 * j:128 * (j + 1)],
                        psum[0:8, 512 * gb2:512 * gb2 + 128], AF.Copy))
                    bank_free[gb2] = (s_s, sm2)
                    wgt_evac[j] = sm2
                if part == "head":
                    return
                emit_tail_evac()

            def emit_tail_evac():
                wgt_all = max(wgt_evac)
                for e in range(E):
                    b0 = (e % 2) * 2
                    bank_acquire(b0)
                    bank_acquire(b0 + 1)
                    t_wait(s_s, wgt_all)
                    wT.wait(s_k, k_done)
                    T.matmul(bank(b0), selB[:, 128 * e:128 * (e + 1)],
                             wgtT[0:8, 0:512], start=True, stop=True)
                    tm = s_t.inc(T.matmul(bank(b0 + 1),
                                          selB[:, 128 * e:128 * (e + 1)],
                                          wgtT[0:8, 512:1024], start=True,
                                          stop=True))
                    bank_ready[b0] = bank_ready[b0 + 1] = (s_t, tm)
                    wS.wait(s_t, tm)
                    if wbc_vguard_ref[0]:
                        wS.wait(s_v, wbc_vguard_ref[0])
                    sm2 = s_s.inc(S.activation(wbc[e][:, :], bank(b0, 1024),
                                               AF.Copy))
                    wbc_mark[e] = sm2
                    bank_free[b0] = bank_free[b0 + 1] = (s_s, sm2)
                ln2t_ready[X] = s_s.n

            return stats_fn, piece_fn, tail_fn

        ln2t_half = [[0, 0], [0, 0]]
        ln2t_ready = [0, 0]

        release = [0] * QC
        v2_rel = [0] * DC
        y_mark = [0] * DC
        wout_mark = [0]
        win_mark = [0]
        ln1T_ready = [0]
        evac_mark = {}
        wout_info = {}

        QORDER = []
        for c in range(DC):
            QORDER += [8 + c, 16 + c]
        QORDER += list(range(DC))

        def emit_ln1(t):
            last = 0
            bm = emit_ln_stats_batch([4 * t + j for j in range(4)])
            for j in range(4):
                nm = emit_normalize(4 * t + j, j, ln1f, bm)
                last = emit_transpose_evac(
                    ln1f, 6,
                    [ln1T[Dc][:, 128 * j:128 * (j + 1)] for Dc in range(DC)],
                    None, nm,
                    evac_waits=((s_t, win_mark[0]),))
            ln1T_ready[0] = last

        def emit_win(t):
            wT.wait(s_s, ln1T_ready[0])
            for i, qc in enumerate(QORDER):
                b = i % 2
                bank_acquire(b)
                warmers(s_v, release[qc])
                t_wait(s_v, release[qc])
                for Dc in range(DC):
                    wT.wait(s_wd[Dc], 16)
                    ins = T.matmul(bank(b),
                                   winB[Dc][:, 128 * qc:128 * (qc + 1)],
                                   ln1T[Dc][:, 0:512],
                                   start=(Dc == 0), stop=(Dc == DC - 1))
                bank_ready[b] = (s_t, s_t.inc(ins))
                wS.wait(*bank_ready[b])
                em = s_s.inc(S.activation(
                    qkvraw[qc][:, 8:520], bank(b), AF.Identity,
                    bias=constsf[:, BIN0 + qc:BIN0 + qc + 1]))
                evac_mark[qc] = em
                bank_free[b] = (s_s, em)
            win_mark[0] = s_t.n
            last_win_mark[0] = s_t.n

        def conv3(dst, rawap, qc, extra_waits=()):
            for w in extra_waits:
                wV.wait(s_v, w)
            wV.wait(s_s, evac_mark[qc])
            wV.wait(s_gm, halo0_done)
            m = s_v.inc(V.scalar_tensor_tensor(
                dst[:, 0:512], rawap[:, 6:518],
                constsf[:, SW0 + 3 * qc:SW0 + 3 * qc + 1],
                constsf[:, SB0 + qc:SB0 + qc + 1].broadcast_to([128, 512]),
                ALU.mult, ALU.add))
            for k in (1, 2):
                wV.wait(s_v, m)
                m = s_v.inc(V.scalar_tensor_tensor(
                    dst[:, 0:512], rawap[:, 6 + k:518 + k],
                    constsf[:, SW0 + 3 * qc + k:SW0 + 3 * qc + k + 1],
                    dst[:, 0:512], ALU.mult, ALU.add))
            return m

        def halo_copy(qc, skip=False):
            wV.wait(s_s, evac_mark[qc])
            if skip:
                return s_v.n
            return s_v.inc(V.tensor_copy(qkvraw[qc][:, 4:8],
                                         qkvraw[qc][:, 516:520]))

        def emit_conv(t, chunk_cb=None):
            last_tile = (t == ATILES - 1)
            def tap(dst, rawap, qc, k, prev):
                if prev:
                    wV.wait(s_v, prev)
                if k == 0:
                    wV.wait(s_s, evac_mark[qc])
                    wV.wait(s_gm, halo0_done)
                    return s_v.inc(V.scalar_tensor_tensor(
                        dst[:, 0:512], rawap[:, 6:518],
                        constsf[:, SW0 + 3 * qc:SW0 + 3 * qc + 1],
                        constsf[:, SB0 + qc:SB0 + qc + 1].broadcast_to(
                            [128, 512]),
                        ALU.mult, ALU.add))
                return s_v.inc(V.scalar_tensor_tensor(
                    dst[:, 0:512], rawap[:, 6 + k:518 + k],
                    constsf[:, SW0 + 3 * qc + k:SW0 + 3 * qc + k + 1],
                    dst[:, 0:512], ALU.mult, ALU.add))

            def itap(dst, c, k, prev):
                wV.wait(s_v, prev)
                if k == 0:
                    return s_v.inc(V.scalar_tensor_tensor(
                        dst[:, 0:512], v2[c][:, 5:517],
                        constsf[:, IW0 + 4 * c:IW0 + 4 * c + 1],
                        constsf[:, IB0 + c:IB0 + c + 1].broadcast_to(
                            [128, 512]),
                        ALU.mult, ALU.add))
                return s_v.inc(V.scalar_tensor_tensor(
                    dst[:, 0:512], v2[c][:, 5 + k:517 + k],
                    constsf[:, IW0 + 4 * c + k:IW0 + 4 * c + k + 1],
                    dst[:, 0:512], ALU.mult, ALU.add))

            temps = [tmpA, tmpB, tmpC]
            tfree = [0, 0, 0]
            for c in range(DC):
                ta = temps[(2 * c) % 3]
                tb = temps[(2 * c + 1) % 3]
                ma = tfree[(2 * c) % 3]
                mb = tfree[(2 * c + 1) % 3]
                for k in range(3):
                    ma = tap(ta, qkvraw[8 + c], 8 + c, k, ma)
                    mb = tap(tb, qkvraw[16 + c], 16 + c, k, mb)
                release[8 + c] = halo_copy(8 + c)
                release[16 + c] = halo_copy(16 + c)
                wV.wait(s_v, ma)
                wV.wait(s_v, mb)
                wV.wait(s_v, v2_rel[c])
                mv = s_v.inc(V.tensor_tensor(v2[c][:, 8:520], ta[:, 0:512],
                                             tb[:, 0:512], ALU.mult))
                mi = mq = mv
                for k in range(4):
                    mi = itap(tb, c, k, mi)
                    if k < 3:
                        mq = tap(ta, qkvraw[c], c, k, mq)
                release[c] = halo_copy(c)
                wV.wait(s_v, mi)
                v2_rel[c] = s_v.inc(V.tensor_copy(v2[c][:, 4:8],
                                                  v2[c][:, 516:520]))
                wV.wait(s_v, mq)
                wV.wait(s_t, wout_mark[0])
                ym = s_v.inc(V.tensor_tensor(ybuf[c][:, :], ta[:, 0:512],
                                             tb[:, 0:512], ALU.mult))
                y_mark[c] = ym
                tfree[(2 * c) % 3] = tfree[(2 * c + 1) % 3] = ym
                if chunk_cb is not None:
                    chunk_cb(c)

        def emit_wout_pair(t, js):
            wT.wait(s_gw, wout_done)
            wT.wait(s_gw, bout_done)
            wT.wait(s_k, k_done)
            for j in js:
                tokc = 4 * t + j
                bb = 2 + 2 * (j % 2)
                bank_acquire(bb)
                bank_acquire(bb + 1)
                T.matmul(bank(bb), onesB[:, :], boutB[:, 0:512],
                         start=True, stop=False)
                T.matmul(bank(bb + 1), onesB[:, :], boutB[:, 512:1024],
                         start=True, stop=False)
                for c in range(DC):
                    warmers(s_v, y_mark[c])
                    t_wait(s_v, y_mark[c])
                    T.matmul(bank(bb), ybuf[c][:, 128 * j:128 * (j + 1)],
                             woutB[c][:, 0:512], start=False,
                             stop=(c == DC - 1))
                    ins = T.matmul(bank(bb + 1),
                                   ybuf[c][:, 128 * j:128 * (j + 1)],
                                   woutB[c][:, 512:1024], start=False,
                                   stop=(c == DC - 1))
                gm = s_t.inc(ins)
                bank_ready[bb] = bank_ready[bb + 1] = (s_t, gm)
                wout_info[(t, j)] = (bb, gm, tokc)
            wout_mark[0] = s_t.n

        def emit_resid_pair(t, js):
            for j in js:
                bb, gm, tokc = wout_info.pop((t, j))
                rt = resid[:, 1024 * tokc:1024 * (tokc + 1)]
                wV.wait(s_t, gm)
                wV.wait(s_v, resid_mark[tokc])
                rm = s_v.inc(V.tensor_tensor(rt, rt, bank(bb, 1024), ALU.add))
                resid_mark[tokc] = rm
                bank_free[bb] = bank_free[bb + 1] = (s_v, rm)

        last_win_mark = [0]
        g0h = {}
        emit_ln1(0)
        emit_win(0)
        for t in range(ATILES):
            if t + 1 < ATILES:
                emit_ln1(t + 1)
            if t > 0:
                emit_resid_pair(t - 1, (0, 1))
                emit_wout_pair(t - 1, (2, 3))
            if t == ATILES - 1:
                g0h["fns"] = make_gate_half(
                    0, ln2T_tguard=0, region_tguard=last_win_mark[0],
                    bufs=(ln2ft, ln1f), tfs=(ln2tfA, ln2tfA))
                g0h["fns"][0]()
                for j in range(8):
                    g0h["fns"][1](j)
                g0h["fns"][2]("head")
                emit_conv(t)
            else:
                emit_conv(t)
            if t > 0:
                emit_resid_pair(t - 1, (2, 3))
            if t + 1 < ATILES:
                emit_win(t + 1)
            emit_wout_pair(t, (0, 1))
        tl = ATILES - 1
        emit_resid_pair(tl, (0, 1))
        emit_wout_pair(tl, (2, 3))
        emit_resid_pair(tl, (2, 3))
        attn_v_done = s_v.n
        attn_t_done = s_t.n


        wloads = []
        for X in range(2):
            for e in range(E):
                wloads.append(("w1", X, e))
                wloads.append(("w2", X, e))
        w1_done, w2_done = {}, {}
        h_done_mark, o_done_mark = {}, {}

        def issue_wload(kind, X, e):
            if (X, e) == (0, 0):
                if kind == "w1":
                    wG.wait(s_t, last_win_mark[0])
                else:
                    wG.wait(s_t, attn_t_done)
                    wG.wait(s_v, attn_v_done)
            else:
                pe = (X, e - 1) if e > 0 else (X - 1, E - 1)
                if kind == "w1":
                    wG.wait(s_t, h_done_mark[pe])
                else:
                    wG.wait(s_t, o_done_mark[pe])
            wG.wait(s_gw, s_gw.n)
            if kind == "w1":
                w1_done[(X, e)] = s_gw.inc(G.dma_start(w1_region, d_w1[e]), 16)
            else:
                w2_done[(X, e)] = s_gw.inc(G.dma_start(w2_region, d_w2[e]), 16)

        issue_wload("w1", 0, 0)
        g0h["fns"][2]("evac")

        issue_wload("w2", 0, 0)
        wl_state = [2]

        hs_mark = [[0] * HC, [0] * HC]
        hprev_scale = [[0] * HC, [0] * HC]

        def moe_expert(X, e, after_h_cb=None, o_cb=None, after_o_cb=None):
            wT.wait(s_gw, w1_done[(X, e)])
            for half, (lo, hi) in enumerate(((0, 512), (512, 1024))):
                wT.wait(s_s, ln2t_half[X][half])
                for Hcc in range(HC):
                    bb = (Hcc % 2) * 2 + half
                    bank_acquire(bb)
                    for Dc in range(DC):
                        lhs = w1B[Dc][:, 128 * Hcc:128 * (Hcc + 1)]
                        ins = T.matmul(bank(bb), lhs, ln2T[Dc][:, lo:hi],
                                       start=(Dc == 0), stop=(Dc == DC - 1))
                    gm = s_t.inc(ins)
                    bank_ready[bb] = (s_t, gm)
                    wS.wait(s_t, gm)
                    wS.wait(s_t, attn_t_done)
                    if hprev_scale[half][Hcc]:
                        wS.wait(s_v, hprev_scale[half][Hcc])
                    em = s_s.inc(S.activation(
                        hbuf[Hcc][:, lo:hi], bank(bb), AF.Relu,
                        bias=constsf[:, B10 + 16 * e + Hcc:
                                     B10 + 16 * e + Hcc + 1]))
                    bank_free[bb] = (s_s, em)
                    wV.wait(s_s, em)
                    wV.wait(s_s, wbc_mark[e])
                    hs_mark[half][Hcc] = s_v.inc(V.tensor_tensor(
                        hbuf[Hcc][:, lo:hi], hbuf[Hcc][:, lo:hi],
                        wbc[e][:, lo:hi], ALU.mult))
                    hprev_scale[half][Hcc] = hs_mark[half][Hcc]
            h_done_mark[(X, e)] = s_t.n
            if after_h_cb is not None:
                after_h_cb()
            if wl_state[0] < len(wloads):
                issue_wload(*wloads[wl_state[0]])
                wl_state[0] += 1
            wT.wait(s_gw, w2_done[(X, e)])
            for jj in range(8):
                tokc = 8 * X + jj
                bb = 4 + (jj % 2) * 2
                bank_acquire(bb)
                bank_acquire(bb + 1)
                if e == 0:
                    wT.wait(s_gw, b2_done)
                    wT.wait(s_s, wgt_evac[jj])
                    T.matmul(bank(bb), wgtT[0:8, 128 * jj:128 * (jj + 1)],
                             b2B[0:8, 0:512], start=True, stop=False)
                    T.matmul(bank(bb + 1), wgtT[0:8, 128 * jj:128 * (jj + 1)],
                             b2B[0:8, 512:1024], start=True, stop=False)
                for Hcc in range(HC):
                    t_wait(s_v, hs_mark[jj // 4][Hcc])
                    lhs = hbuf[Hcc][:, 128 * jj:128 * (jj + 1)]
                    st = (e != 0) and (Hcc == 0)
                    T.matmul(bank(bb), lhs, w2B[Hcc][:, 0:512],
                             start=st, stop=(Hcc == HC - 1))
                    ins = T.matmul(bank(bb + 1), lhs, w2B[Hcc][:, 512:1024],
                                   start=st, stop=(Hcc == HC - 1))
                gm = s_t.inc(ins)
                bank_ready[bb] = bank_ready[bb + 1] = (s_t, gm)
                rt = resid[:, 1024 * tokc:1024 * (tokc + 1)]
                wV.wait(s_t, gm)
                wV.wait(s_v, resid_mark[tokc])
                rm = s_v.inc(V.tensor_tensor(rt, rt, bank(bb, 1024), ALU.add))
                resid_mark[tokc] = rm
                bank_free[bb] = bank_free[bb + 1] = (s_v, rm)
                if e == E - 1:
                    wSy.wait(s_v, rm)
                    wSy.wait(s_o, s_o.n)
                    s_o.inc(sy.dma_start(
                        d_out[128 * tokc:128 * (tokc + 1), :], rt), 16)
                if o_cb is not None:
                    o_cb(jj)
            if after_o_cb is not None:
                after_o_cb()
            o_done_mark[(X, e)] = s_t.n
            if wl_state[0] < len(wloads):
                issue_wload(*wloads[wl_state[0]])
                wl_state[0] += 1

        for e in range(E - 2):
            moe_expert(0, e)
        g1_holder = {}

        g1_vguard = [0]

        def g1_stats_cb(jj):
            if jj == 0:
                g1_vguard[0] = max(max(hprev_scale[0]), max(hprev_scale[1]))
                g1_holder["fns"] = make_gate_half(
                    1, ln2T_tguard=0,
                    wbc_vguard_ref=g1_vguard,
                    wgtT_tguard=o_done_mark[(0, 0)],
                    bufs=(ln2ft, ln2ft2), tfs=(ln2tfA, ln2tfB))
                g1_holder["fns"][0]()

        moe_expert(0, E - 2, o_cb=g1_stats_cb)

        def g1_after_h():
            g1_vguard[0] = max(max(hprev_scale[0]), max(hprev_scale[1]))

        def g1_piece(jj):
            if jj >= 2:
                g1_holder["fns"][1](jj - 2)

        def g1_tail():
            g1_holder["fns"][1](6)
            g1_holder["fns"][1](7)
            g1_holder["fns"][2]()

        moe_expert(0, E - 1, after_h_cb=g1_after_h, o_cb=g1_piece,
                   after_o_cb=g1_tail)
        for e in range(E):
            moe_expert(1, e)

        wSy.wait(s_o, s_o.n)
        assert s_o.n == 16 * 16, s_o.n

    ctx.close()
    return nc


_PROG_CACHE = {}


def _get_prog():
    if "nc" not in _PROG_CACHE:
        _PROG_CACHE["nc"] = build_program()
    return _PROG_CACHE["nc"]


def kernel(L0_attn_u, L0_attn_W_in, L0_attn_b_in, L0_attn_short_w,
           L0_attn_short_b, L0_attn_inner_w, L0_attn_inner_b, L0_attn_W_out,
           L0_attn_b_out, L0_ffn_gate_w, L0_ffn_gate_b, L0_ffn_w1, L0_ffn_b1,
           L0_ffn_w2, L0_ffn_b2):
    f32 = np.float32
    u = np.ascontiguousarray(np.asarray(L0_attn_u, f32))
    W_in = np.asarray(L0_attn_W_in, f32)
    b_in = np.asarray(L0_attn_b_in, f32)
    short_w = np.asarray(L0_attn_short_w, f32)
    short_b = np.asarray(L0_attn_short_b, f32)
    inner_w = np.asarray(L0_attn_inner_w, f32)
    inner_b = np.asarray(L0_attn_inner_b, f32)
    W_out = np.asarray(L0_attn_W_out, f32)
    b_out = np.asarray(L0_attn_b_out, f32)
    gate_w = np.asarray(L0_ffn_gate_w, f32)
    gate_b = np.asarray(L0_ffn_gate_b, f32)
    w1 = np.asarray(L0_ffn_w1, f32)
    b1 = np.asarray(L0_ffn_b1, f32)
    w2 = np.asarray(L0_ffn_w2, f32)
    b2 = np.asarray(L0_ffn_b2, f32)

    winl = np.ascontiguousarray(
        W_in.reshape(8, 128, 3072).transpose(1, 0, 2).reshape(128, 24576))
    woutl = np.ascontiguousarray(
        W_out.reshape(8, 128, 1024).transpose(1, 0, 2).reshape(128, 8192))
    w1l = np.ascontiguousarray(
        w1.reshape(8, 8, 128, 2048).transpose(0, 2, 1, 3).reshape(8, 128, 16384))
    w2l = np.ascontiguousarray(
        w2.reshape(8, 16, 128, 1024).transpose(0, 2, 1, 3).reshape(8, 128, 16384))
    cf = np.zeros((128, NCONST + 128), f32)
    cf[:, SW0:SW0 + 72] = short_w.reshape(24, 128, 3).transpose(1, 0, 2).reshape(128, 72)
    cf[:, SB0:SB0 + 24] = short_b.reshape(24, 128).T
    cf[:, IW0:IW0 + 32] = inner_w.reshape(8, 128, 4).transpose(1, 0, 2).reshape(128, 32)
    cf[:, IB0:IB0 + 8] = inner_b.reshape(8, 128).T
    cf[:, BIN0:BIN0 + 24] = b_in.reshape(24, 128).T
    cf[:, B10:B10 + 128] = b1.reshape(8, 16, 128).transpose(2, 0, 1).reshape(128, 128)
    cf[:, GW0:GW0 + 64] = gate_w.reshape(8, 8, 128).transpose(2, 1, 0).reshape(128, 64)
    cf[:, GB0:GB0 + 8] = np.broadcast_to(gate_b, (128, 8))
    cf[:, NCONST:NCONST + 128] = np.eye(128, dtype=f32)
    b2r = np.ascontiguousarray(b2)
    boutr = np.ascontiguousarray(b_out.reshape(1, 1024))
    selb = np.zeros((8, 1152), ml_dtypes.bfloat16)
    for e in range(8):
        selb[e, 128 * e:128 * (e + 1)] = 1
    selb[0, 1024:1152] = 1

    shared = dict(winl=winl, woutl=woutl, w1l=w1l, w2l=w2l, constsf=cf,
                  b2r=b2r, boutr=boutr, selb=selb)
    in_maps = [dict(u=np.ascontiguousarray(u[c]), **shared)
               for c in range(NCORES)]

    nc = _get_prog()
    res = run_bass_kernel_spmd(nc, in_maps, core_ids=list(range(NCORES)))
    out = np.stack([res.results[c]["out"] for c in range(NCORES)], axis=0)
    return np.ascontiguousarray(out.astype(np.float32))
